# version 29
# baseline (speedup 1.0000x reference)
import numpy as np
import concourse.bass as bass
import concourse.mybir as mybir
from concourse.bass_utils import run_bass_kernel_spmd

F32 = mybir.dt.float32
BF16 = mybir.dt.bfloat16
AF = mybir.ActivationFunctionType
ALU = mybir.AluOpType

NCORES = 8
T = 1024
HALO = 512
A = T + HALO
D = 2048
NH, NKV, HD, SUB, WIN = 16, 4, 128, 64, 512
DFF = 4096
NTT = A // 128
NKC = D // 128
NEG = -1e30
EXPB = -16.0
SCALE = 0.125

ARENA = 203 * 1024
SLOT = 512

OFFS = {
    "xnT":   (0,      NKC * A * 2),
    "wstr":  (49152,  16384),
    "qfm":   (65536,  NH * T * 2),
    "kfm":   (98304,  NKV * A * 2),
    "vtm":   (110592, NTT * 512 * 2),
    "scr":   (122880, 18432),
    "rope":  (141312, 2 * A * 4),
    "e":     (141312, 2 * 1280 * 4),
    "attn":  (153600, NH * T * 2),
    "outac": (65536,  NKC * T * 4),
    "h1":    (131072, 32 * T * 2),
    "h2":    (0,      16 * T * 2),
    "w2s":   (32768,  16384),
    "w3s":   (49152,  16384),
    "outs":  (131072, 16384),
    "band":  (196608, 1280 * 2),
    "combo": (199168, 512 * 2),
    "smalls": (200192, 2560),
}
SCR = OFFS["scr"][0]
OFF_QTMP = [SCR, SCR + 6144]
OFF_RTMP = SCR + 12288
OFF_WTS = SCR
OFF_WP = [SCR + 8192, SCR + 8192 + 2560]
OFF_WBF = [SCR + 13312, SCR + 13312 + 1280]
OFF_DIF = [SCR + 15872, SCR + 15872 + 1280]

DMA_ENGS = ("syncq", "gpq")


class R:
    __slots__ = ("w", "rs")

    def __init__(self):
        self.w = None
        self.rs = []


class Sched:

    def __init__(self, nc):
        self.nc = nc
        self.sems = {}
        self.cnt = {}
        self.ops = []
        self.slots = [R() for _ in range((ARENA // SLOT) + 2)]
        self.pbank = [R() for _ in range(8)]

    def sem(self, key):
        if key not in self.sems:
            self.sems[key] = self.nc.alloc_semaphore(key + "_sem")
            self.cnt[key] = 0
        return self.sems[key]

    def srange(self, off, nbytes):
        return self.slots[off // SLOT: (off + nbytes + SLOT - 1) // SLOT]

    def em(self, eng, fn, reads=(), writes=(), key=None):
        key = key or eng
        self.sem(key)
        waits = {}

        def need(dep):
            dkey, dcnt = dep
            if dkey == key == "pe":
                return
            if waits.get(dkey, -1) < dcnt:
                waits[dkey] = dcnt
        rset = []
        for r in reads:
            rset.extend(r if isinstance(r, (list, tuple)) else [r])
        wset = []
        for r in writes:
            wset.extend(r if isinstance(r, (list, tuple)) else [r])
        for r in rset:
            if r.w:
                need(r.w)
        for r in wset:
            if r.w:
                need(r.w)
            for rr in r.rs:
                need(rr)
        inc = 16 if eng in DMA_ENGS else 1
        self.cnt[key] += inc
        my = (key, self.cnt[key])
        for r in rset:
            r.rs.append(my)
        for r in wset:
            r.w = my
            r.rs = []
        self.ops.append((eng, key, fn, waits, inc))

    def replay(self, engkeys, e):
        waited = {}
        for eng, key, fn, waits, inc in self.ops:
            if eng not in engkeys:
                continue
            for sname, val in waits.items():
                if waited.get(sname, -1) < val:
                    e.wait_ge(self.sems[sname], val)
                    waited[sname] = val
            inst = fn(e)
            inst.then_inc(self.sems[key], inc)


def build(debug=False):
    nc = bass.Bass("TRN2", target_bir_lowering=False, debug=False)

    dram = {}
    def din(name, shape, dtype=F32):
        dram[name] = nc.dram_tensor(name, shape, dtype, kind="ExternalInput").ap()
        return dram[name]

    xh = din("xh", [A, D])
    Wq = din("Wq", [D, NH * HD]); Wk = din("Wk", [D, NKV * HD]); Wv = din("Wv", [D, NKV * HD])
    Wo = din("Wo", [NH * HD, D])
    W1T = din("W1T", [D, DFF]); W2T = din("W2T", [DFF, DFF]); W3T = din("W3T", [DFF, D])
    b1t = din("b1t", [128, 32]); b2t = din("b2t", [128, 32]); b3t = din("b3t", [128, 16])
    lam_bc = din("lam_bc", [128, 16])
    ropecos = din("ropecos", [128, A]); ropesinm = din("ropesinm", [128, A])
    band_d = din("band", [128, 1280], BF16)
    combo_d = din("combo", [128, 512], BF16)
    y = nc.dram_tensor("y", [T, D], F32, kind="ExternalOutput").ap()
    dbg = {}
    if debug:
        for nm, shape, dt in [
            ("dbg_xnT", [128, NKC * A], BF16), ("dbg_q", [128, NH * T], BF16),
            ("dbg_k", [128, NKV * A], BF16), ("dbg_v", [128, NTT * 512], BF16),
            ("dbg_attn", [128, NH * T], BF16), ("dbg_wo", [128, NKC * T], F32),
            ("dbg_h1", [128, 32 * T], BF16),
            ("dbg_sp", [128, 1280], F32), ("dbg_e", [128, 1280], F32),
            ("dbg_sc", [128, 8], F32), ("dbg_wp", [128, 640], F32),
            ("dbg_dif", [128, 640], BF16), ("dbg_wts", [128, 4096], BF16),
        ]:
            dbg[nm] = nc.dram_tensor(nm, shape, dt, kind="ExternalOutput").ap()

    arena = nc.alloc_sbuf_tensor("arena", [128, ARENA], mybir.dt.uint8)
    base = None
    for alloc in nc.m.functions[0].allocations:
        try:
            ml = alloc.memorylocations[0]
            if ml.name == "arena":
                base = ml.addr
        except Exception:
            pass
    assert base is not None

    def sb(name, shape, dtype, off):
        t = nc.alloc_sbuf_tensor_at(name, shape, dtype, offset=base + off)
        return t.ap()

    xnT = sb("xnT", [128, NKC * A], BF16, OFFS["xnT"][0])
    XF_OFFS = [OFFS["wstr"][0], OFFS["wstr"][0] + 8192,
               OFFS["qfm"][0], OFFS["qfm"][0] + 8192, OFFS["qfm"][0] + 16384]
    xf = [sb(f"xf{i}", [128, D], F32, XF_OFFS[i]) for i in range(5)]
    xnb4 = [sb(f"xnb{i}", [128, D], BF16, OFFS["qfm"][0] + 24576 + 4096 * i) for i in range(4)]
    nmr = [sb(f"nmr{i}", [128, 1], F32, OFFS["smalls"][0] + 2496 + 32 * i) for i in range(2)]
    wkq_buf = [sb(f"wkq{i}", [128, NKC * 256], BF16, OFFS["wstr"][0] + i * 8192) for i in range(2)]
    wv_buf = sb("wvb", [128, NKC * 512], BF16, OFFS["wstr"][0])
    wo_buf = [sb(f"wob{i}", [128, NKC * 256], BF16, OFFS["wstr"][0] + i * 8192) for i in range(2)]
    w1_buf = [sb(f"w1b{i}", [128, NKC * 256], BF16, OFFS["wstr"][0] + i * 8192) for i in range(2)]
    qfm = sb("qfm", [128, NH * T], BF16, OFFS["qfm"][0])
    kfm = sb("kfm", [128, NKV * A], BF16, OFFS["kfm"][0])
    vtm = sb("vtm", [128, NTT * 512], BF16, OFFS["vtm"][0])
    qtmp = [sb(f"qtmp{i}", [128, A], F32, OFF_QTMP[i]) for i in range(2)]
    rtmp = sb("rtmp", [128, A], F32, OFF_RTMP)
    wts = sb("wts", [128, 8 * 512], BF16, OFF_WTS)
    wp = [sb(f"wp{i}", [128, 640], F32, OFF_WP[i]) for i in range(2)]
    WBF_OFFS = [OFF_WBF[0], OFF_WBF[1], OFFS["h1"][0] + 55296, OFFS["h1"][0] + 55296 + 1280]
    wbf = [sb(f"wbf{i}", [128, 640], BF16, WBF_OFFS[i]) for i in range(4)]
    PSCR_OFFS = [OFFS["h1"][0] + 57856, OFFS["h1"][0] + 59136]
    pscr = [sb(f"pscr{i}", [128, 256], F32, PSCR_OFFS[i]) for i in range(2)]
    dif = [sb(f"dif{i}", [128, 640], BF16, OFF_DIF[i]) for i in range(2)]
    ropec = sb("ropec", [128, A], F32, OFFS["rope"][0])
    ropes = sb("ropes", [128, A], F32, OFFS["rope"][0] + A * 4)
    ebuf = [sb(f"ebuf{i}", [128, 1280], F32, OFFS["e"][0] + i * 5120) for i in range(2)]
    attn = sb("attn", [128, NH * T], BF16, OFFS["attn"][0])
    outac = sb("outac", [128, NKC * T], F32, OFFS["outac"][0])
    h1 = sb("h1", [128, 32 * T], BF16, OFFS["h1"][0])
    h2 = sb("h2", [128, 16 * T], BF16, OFFS["h2"][0])
    w2_buf = [sb(f"w2b{i}", [128, 32 * 128], BF16, OFFS["w2s"][0] + i * 8192) for i in range(2)]
    w3_buf = [sb(f"w3b{i}", [128, 16 * 256], BF16, OFFS["w3s"][0] + i * 8192) for i in range(2)]
    outs = [sb(f"outs{i}", [128, D], F32, OFFS["outs"][0] + i * 8192) for i in range(2)]
    band = sb("bandt", [128, 1280], BF16, OFFS["band"][0])
    combo = sb("combot", [128, 512], BF16, OFFS["combo"][0])
    SM = OFFS["smalls"][0]
    b1s = sb("b1s", [128, 32], F32, SM + 0)
    b2s = sb("b2s", [128, 32], F32, SM + 128)
    b3s = sb("b3s", [128, 16], F32, SM + 256)
    neglam = sb("neglam", [128, 16], F32, SM + 320)
    lam_s = sb("lam_s", [128, 16], F32, SM + 384)
    ident = sb("ident", [128, 128], BF16, SM + 448)
    ident32 = sb("ident32", [128, 128], F32, SM + 704)
    bnst = [sb(f"bnst{i}", [128, 24], F32, SM + 1216 + 96 * i) for i in range(2)]
    mv = [sb(f"mv{i}", [128, 2], F32, SM + 2240 + 32 * i) for i in range(2)]
    ln_s = [sb(f"ln_s{i}", [128, 6], F32, SM + 2304 + 32 * i) for i in range(2)]
    S1 = [sb(f"S1_{i}", [128, 1], F32, SM + 1376 + 32 * i) for i in range(4)]
    S2 = [sb(f"S2_{i}", [128, 1], F32, SM + 1504 + 32 * i) for i in range(4)]
    gg = [sb(f"gg_{i}", [128, 1], F32, SM + 1632 + 32 * i) for i in range(2)]
    r2s = [sb(f"r2_{i}", [128, 1], F32, SM + 1696 + 32 * i) for i in range(2)]
    ngc = [sb(f"ngc_{i}", [128, 1], F32, SM + 1760 + 32 * i) for i in range(2)]
    rv2 = [sb(f"rv2_{i}", [128, 1], F32, SM + 1824 + 32 * i) for i in range(2)]
    tds = [sb(f"tds_{i}", [128, 1], F32, SM + 1888 + 32 * i) for i in range(2)]
    expb = sb("expb", [128, 1], F32, SM + 2016)
    eps_t = sb("eps_t", [128, 1], F32, SM + 2048)
    bnst2 = [sb(f"bnst2{i}", [128, 24], F32, SM + 2080 + 0 * i) for i in range(1)]
    bnst2b = sb("bnst2b", [128, 24], F32, SM + 2368)
    bnst2 = [bnst2[0], bnst2b]
    tds2 = [sb(f"tds2_{i}", [128, 1], F32, SM + 1952 + 32 * i) for i in range(2)]

    pm_h = nc.alloc_psum_tensor("pm", [128, 4096], F32)
    pm = pm_h.ap()

    s = Sched(nc)

    res_cache = {}
    def rs(name, off, nbytes):
        key = (off, nbytes)
        if key not in res_cache:
            res_cache[key] = s.srange(off, nbytes)
        return res_cache[key]

    R_xnT_all = rs("xnTall", OFFS["xnT"][0], OFFS["xnT"][1])
    R_wstr = [rs("wstr", OFFS["wstr"][0] + i * 8192, 8192) for i in range(2)]
    R_xf = [rs("xf", XF_OFFS[i], 8192) for i in range(5)]
    R_xnb = [rs("xnb", OFFS["qfm"][0] + 24576 + 4096 * i, 4096) for i in range(4)]
    R_q = [rs("q", OFFS["qfm"][0] + h * T * 2, T * 2) for h in range(NH)]
    R_k = rs("k", OFFS["kfm"][0], OFFS["kfm"][1])
    R_v = rs("v", OFFS["vtm"][0], OFFS["vtm"][1])
    R_qtmp = [rs("qtmp", OFF_QTMP[i], 6144) for i in range(2)]
    R_rtmp = rs("rtmp", OFF_RTMP, 6144)
    R_wts = rs("wts", OFF_WTS, 8192)
    R_wp = [rs("wp", OFF_WP[i], 2560) for i in range(2)]
    R_wbf = [rs("wbf", WBF_OFFS[i], 1280) for i in range(4)]
    R_pscr = [rs("pscr", PSCR_OFFS[i], 1024) for i in range(2)]
    R_dif = [rs("dif", OFF_DIF[i], 1280) for i in range(2)]
    R_rope = rs("rope", OFFS["rope"][0], OFFS["rope"][1])
    R_e = [rs("e", OFFS["e"][0] + i * 5120, 5120) for i in range(2)]
    R_attn = [rs("attn", OFFS["attn"][0] + h * T * 2, T * 2) for h in range(NH)]
    R_out = [rs("out", OFFS["outac"][0] + c * T * 4, T * 4) for c in range(NKC)]
    R_h1 = [rs("h1", OFFS["h1"][0] + c * T * 2, T * 2) for c in range(32)]
    R_h2 = [rs("h2", OFFS["h2"][0] + c * T * 2, T * 2) for c in range(16)]
    R_w2 = [rs("w2s", OFFS["w2s"][0] + i * 8192, 8192) for i in range(2)]
    R_w3 = [rs("w3s", OFFS["w3s"][0] + i * 8192, 8192) for i in range(2)]
    R_outs = [rs("outs", OFFS["outs"][0] + i * 8192, 8192) for i in range(2)]
    R_band = rs("band", OFFS["band"][0], OFFS["band"][1])
    R_combo = rs("combo", OFFS["combo"][0], OFFS["combo"][1])
    R_sm = rs("smalls", SM, 1216)
    R_lnA = [R() for _ in range(2)]
    R_S1 = [R() for _ in range(4)]
    R_S2 = [R() for _ in range(4)]
    R_gg = [R() for _ in range(2)]
    R_r2 = [R() for _ in range(2)]
    R_ngc = [R() for _ in range(2)]
    R_rv2 = [R() for _ in range(2)]
    R_tds = [R() for _ in range(2)]

    PB = s.pbank

    em = s.em

    em("syncq", lambda e: e.dma_start(out=band, in_=band_d), writes=[R_band], key="kc_band")
    em("syncq", lambda e: e.dma_start(out=combo, in_=combo_d), writes=[R_combo], key="kc_combo")
    em("syncq", lambda e: e.dma_start(out=ropec, in_=ropecos), writes=[R_rope], key="kc_ropec")
    em("syncq", lambda e: e.dma_start(out=ropes, in_=ropesinm), writes=[R_rope], key="kc_ropes")
    em("syncq", lambda e: e.dma_start(out=b1s, in_=b1t), writes=[R_sm], key="kc_b1")
    em("syncq", lambda e: e.dma_start(out=b2s, in_=b2t), writes=[R_sm], key="kc_b2")
    em("syncq", lambda e: e.dma_start(out=b3s, in_=b3t), writes=[R_sm], key="kc_b3")
    em("syncq", lambda e: e.dma_start(out=lam_s, in_=lam_bc), writes=[R_sm], key="kc_lam")
    em("dve", lambda e: e.tensor_scalar(out=neglam, in0=lam_s, scalar1=-1.0,
                                        scalar2=None, op0=ALU.mult),
       reads=[R_sm], writes=[R_sm])
    em("pool", lambda e: e.memset(ident, 1.0), writes=[R_sm])
    em("pool", lambda e: e.affine_select(out=ident, in_=ident, pattern=[[-1, 128]],
                                         base=0, channel_multiplier=1,
                                         compare_op=ALU.is_equal, fill=0.0),
       reads=[R_sm], writes=[R_sm])
    em("dve", lambda e: e.tensor_copy(out=ident32, in_=ident), reads=[R_sm], writes=[R_sm])
    em("dve", lambda e: e.memset(expb, EXPB), writes=[R_sm])
    em("dve", lambda e: e.memset(eps_t, 1e-5), writes=[R_sm])

    paT2 = [pm[:, 3072:4096].bitcast(BF16), pm[:, 0:1024].bitcast(BF16)]
    paTB = [[PB[6], PB[7]], [PB[0], PB[1]]]
    xnT3 = xnT.rearrange("p (c t) -> p c t", t=A)

    def emA_load(tt):
        p5 = tt % 5
        em("syncq", lambda e, tt=tt, p5=p5: e.dma_start(
            out=xf[p5], in_=xh[128 * tt:128 * (tt + 1), :]),
           writes=[R_xf[p5]], key=f"kx{p5}")

    def emA_stats(tt):
        p5, p = tt % 5, tt % 2
        for kseg in range(4):
            em("dve", lambda e, p=p, p5=p5, kseg=kseg: e.bn_stats(
                out=bnst[p][:, 6 * kseg:6 * kseg + 6], in_=xf[p5][:, 512 * kseg:512 * (kseg + 1)]),
               reads=[R_xf[p5]], writes=[R_lnA[p]])

    def emA_bounce(tt):
        p = tt % 2
        em("act", lambda e, p=p: e.activation(out=bnst2[p], in_=bnst[p], func=AF.Identity),
           reads=[R_lnA[p]], writes=[R_lnA[p]])

    def emA_aggr(tt):
        p = tt % 2
        em("dve", lambda e, p=p: e.bn_aggr(out=mv[p], in_=bnst2[p]),
           reads=[R_lnA[p]], writes=[R_lnA[p]])

    def emA_sqrt(tt):
        p = tt % 2
        em("act", lambda e, p=p: e.activation(out=ln_s[p][:, 1:2], in_=mv[p][:, 1:2], func=AF.Sqrt,
                                              bias=eps_t, scale=1.0),
           reads=[R_lnA[p], R_sm], writes=[R_lnA[p]])

    def emA_norm(tt):
        p5, p4, p = tt % 5, tt % 4, tt % 2
        em("dve", lambda e, p=p: e.reciprocal(out=ln_s[p][:, 2:3], in_=ln_s[p][:, 1:2]),
           reads=[R_lnA[p]], writes=[R_lnA[p]])
        em("dve", lambda e, p=p: e.tensor_scalar(
            out=nmr[p], in0=mv[p][:, 0:1], scalar1=ln_s[p][:, 2:3], scalar2=-1.0,
            op0=ALU.mult, op1=ALU.mult),
           reads=[R_lnA[p]], writes=[R_lnA[p]])
        em("pool", lambda e, p=p, p4=p4, p5=p5: e.tensor_scalar(
            out=xnb4[p4], in0=xf[p5], scalar1=ln_s[p][:, 2:3], scalar2=nmr[p],
            op0=ALU.mult, op1=ALU.add),
           reads=[R_xf[p5], R_lnA[p]], writes=[R_xnb[p4]])

    def emA_transp(tt):
        p4, p = tt % 4, tt % 2

        def transposes(e, p=p, p4=p4):
            last = None
            for c in range(NKC):
                last = e.transpose(paT2[p][:, 128 * c:128 * (c + 1)],
                                   xnb4[p4][:, 128 * c:128 * (c + 1)], ident)
            return last
        em("pe", transposes, reads=[R_xnb[p4], R_sm], writes=paTB[p])

    def emA_evac(tt):
        p = tt % 2
        em("act", lambda e, tt=tt, p=p: e.activation(
            out=xnT3[:, :, 128 * tt:128 * (tt + 1)],
            in_=paT2[p].rearrange("p (c t) -> p c t", t=128), func=AF.Identity),
           reads=paTB[p], writes=[R_xnT_all])

    for it in range(NTT + 7):
        if 0 <= it - 2 < NTT:
            emA_stats(it - 2)
        if 0 <= it - 3 < NTT:
            emA_bounce(it - 3)
        if 0 <= it - 4 < NTT:
            emA_aggr(it - 4)
            emA_sqrt(it - 4)
        if 0 <= it - 5 < NTT:
            emA_norm(it - 5)
        if 0 <= it - 6 < NTT:
            emA_transp(it - 6)
        if 0 <= it - 7 < NTT:
            emA_evac(it - 7)
        if it < NTT:
            emA_load(it)

    if debug:
        em("syncq", lambda e: e.dma_start(out=dbg["dbg_xnT"], in_=xnT), reads=[R_xnT_all], key="kdbg")

    def stream_w(buf_aps, buf_res, dram_ap, chunks, cols):
        def dma(ci):
            bi = ci % 2
            em("gpq", lambda e, ci=ci, bi=bi: e.dma_start(
                out=buf_aps[bi].rearrange("p (kc f) -> p kc f", f=cols),
                in_=dram_ap[:, cols * ci:cols * (ci + 1)].rearrange("(kc p) f -> p kc f", p=128)),
               writes=[buf_res[bi]], key=f"kw{bi}")
        return dma

    kq3 = [wkq_buf[i].rearrange("p (kc f) -> p kc f", f=256) for i in range(2)]
    dma_wk = stream_w(wkq_buf, R_wstr, Wk, 2, 256)
    dma_wk(0); dma_wk(1)
    for kb in range(2):
        bi = kb % 2
        for sub in range(2):
            kvh = 2 * kb + sub
            pq = pm[:, 0:1536] if (kvh % 2 == 0) else pm[:, 1536:3072]
            pqb = [PB[0], PB[1], PB[2]] if (kvh % 2 == 0) else [PB[3], PB[4], PB[5]]

            def kmm(e, bi=bi, sub=sub, pq=pq):
                last = None
                for kc in range(NKC):
                    for w in range(3):
                        last = e.matmul(pq[:, 512 * w:512 * (w + 1)],
                                        kq3[bi][:, kc, 128 * sub:128 * (sub + 1)],
                                        xnT3[:, kc, 512 * w:512 * (w + 1)],
                                        start=(kc == 0), stop=(kc == NKC - 1))
                return last
            em("pe", kmm, reads=[R_wstr[bi], R_xnT_all], writes=pqb)
            em("act", lambda e, pq=pq: e.activation(out=qtmp[0][:, 0:A], in_=pq, func=AF.Identity),
               reads=pqb, writes=[R_qtmp[0]])

            def kshift(e, pq=pq):
                e.activation(out=rtmp[0:64, 0:A], in_=pq[64:128, :], func=AF.Identity)
                return e.activation(out=rtmp[64:128, 0:A], in_=pq[0:64, :], func=AF.Identity)
            em("act", kshift, reads=pqb, writes=[R_rtmp])

            def krope1(e):
                qt = qtmp[0]
                e.tensor_tensor(out=rtmp[:, 0:A], in0=rtmp[:, 0:A],
                                in1=ropes, op=ALU.mult)
                return e.tensor_tensor(out=qt[:, 0:A], in0=qt[:, 0:A], in1=ropec, op=ALU.mult)
            em("dve", krope1, reads=[R_qtmp[0], R_rtmp, R_rope], writes=[R_qtmp[0], R_rtmp])
            em("dve", lambda e, kvh=kvh: e.tensor_tensor(
                out=kfm[:, A * kvh:A * (kvh + 1)], in0=qtmp[0][:, 0:A],
                in1=rtmp, op=ALU.add),
               reads=[R_qtmp[0], R_rtmp], writes=[R_k])

    dma_wq = stream_w(wkq_buf, R_wstr, Wq, 8, 256)
    dma_wq(0); dma_wq(1)
    for qb in range(8):
        bi = qb % 2
        for sub in range(2):
            h = 2 * qb + sub
            par = h % 2
            pq = pm[:, 3072:4096] if par == 0 else pm[:, 0:1024]
            pqb = [PB[6], PB[7]] if par == 0 else [PB[0], PB[1]]

            def qmm(e, bi=bi, sub=sub, pq=pq):
                last = None
                for kc in range(NKC):
                    for w in range(2):
                        last = e.matmul(pq[:, 512 * w:512 * (w + 1)],
                                        kq3[bi][:, kc, 128 * sub:128 * (sub + 1)],
                                        xnT3[:, kc, HALO + 512 * w:HALO + 512 * (w + 1)],
                                        start=(kc == 0), stop=(kc == NKC - 1))
                return last
            em("pe", qmm, reads=[R_wstr[bi], R_xnT_all], writes=pqb)
            em("act", lambda e, pq=pq, par=par: e.activation(
                out=qtmp[par][:, 0:T], in_=pq, func=AF.Identity),
               reads=pqb, writes=[R_qtmp[par]])

            def qshift(e, pq=pq):
                e.activation(out=rtmp[0:64, 0:T], in_=pq[64:128, :], func=AF.Identity)
                return e.activation(out=rtmp[64:128, 0:T], in_=pq[0:64, :], func=AF.Identity)
            em("act", qshift, reads=pqb, writes=[R_rtmp])

            def qrope1(e, par=par):
                qt = qtmp[par]
                e.tensor_tensor(out=rtmp[:, 0:T], in0=rtmp[:, 0:T],
                                in1=ropes[:, HALO:A], op=ALU.mult)
                return e.tensor_tensor(out=qt[:, 0:T], in0=qt[:, 0:T],
                                       in1=ropec[:, HALO:A], op=ALU.mult)
            em("dve", qrope1, reads=[R_qtmp[par], R_rtmp, R_rope], writes=[R_qtmp[par], R_rtmp])
            em("dve", lambda e, h=h, par=par: e.tensor_tensor(
                out=qfm[:, T * h:T * (h + 1)], in0=qtmp[par][:, 0:T],
                in1=rtmp[:, 0:T], op=ALU.add),
               reads=[R_qtmp[par], R_rtmp], writes=[R_q[h]])
        if qb + 2 < 8:
            dma_wq(qb + 2)

    wv3 = wv_buf.rearrange("p (kc f) -> p kc f", f=512)
    em("gpq", lambda e: e.dma_start(
        out=wv3, in_=Wv.rearrange("(kc p) f -> p kc f", p=128)),
       writes=[R_wstr[0], R_wstr[1]], key="kwv")
    for tt in range(NTT):
        bank = tt % 6
        pv = pm[:, 512 * bank:512 * (bank + 1)]

        def vmm(e, tt=tt, pv=pv):
            last = None
            for kc in range(NKC):
                last = e.matmul(pv, xnT3[:, kc, 128 * tt:128 * (tt + 1)],
                                wv3[:, kc, :], start=(kc == 0), stop=(kc == NKC - 1))
            return last
        em("pe", vmm, reads=[R_wstr[0], R_wstr[1], R_xnT_all], writes=[PB[bank]])
        em("act", lambda e, tt=tt, pv=pv: e.activation(
            out=vtm[:, 512 * tt:512 * (tt + 1)], in_=pv, func=AF.Identity),
           reads=[PB[bank]], writes=[R_v])

    if debug:
        em("syncq", lambda e: e.dma_start(out=dbg["dbg_q"], in_=qfm), reads=R_q, key="kdbg")
        em("syncq", lambda e: e.dma_start(out=dbg["dbg_k"], in_=kfm), reads=[R_k], key="kdbg")
        em("syncq", lambda e: e.dma_start(out=dbg["dbg_v"], in_=vtm), reads=[R_v], key="kdbg")

    em("pool", lambda e: e.memset(wts, 0.0), writes=[R_wts])
    insts = [(qb, h, j) for qb in range(2) for h in range(NH) for j in range(4)]
    N = len(insts)
    NDBG = 5
    dbg_sc_t = sb("dbg_sc_t", [128, 8], F32, SM + 2176) if debug else None

    def sp_of(par):
        b0 = 0 if par == 0 else 1536
        return pm[:, b0:b0 + 1280], ([PB[0], PB[1], PB[2]] if par == 0 else [PB[3], PB[4], PB[5]])

    WIN_SPLITS = [(0, 0, 512), (0, 512, 128), (640, 0, 384), (640, 384, 256)]

    def emit_scores(n):
        qb, h, j = insts[n]
        qt = 4 * qb + j
        par = n % 2
        sp, spb = sp_of(par)
        kvh = h // 4
        k0 = 128 * qt
        q0 = T * h + 128 * qt

        def f(e, q0=q0, kvh=kvh, k0=k0, sp=sp):
            last = None
            for sbase, koff, nk in WIN_SPLITS:
                dlo = 0 if sbase == 0 else SUB
                last = e.matmul(
                    sp[:, sbase + koff:sbase + koff + nk],
                    qfm[dlo:dlo + SUB, q0:q0 + 128],
                    kfm[dlo:dlo + SUB, A * kvh + k0 + koff:A * kvh + k0 + koff + nk],
                    start=True, stop=True)
            return last
        em("pe", f, reads=[R_q[h], R_k], writes=spb)

    def emit_mask(n):
        qb, h, j = insts[n]
        qt = 4 * qb + j
        par = n % 2
        sp, spb = sp_of(par)

        def f(e, qt=qt, sp=sp):
            spv = sp.rearrange("p (s c) -> p s c", c=640)
            if qt < 4:
                L = 512 - 128 * qt
                sv = spv[:, :, 0:L]
                cv = combo[:, 0:L].unsqueeze(1).broadcast_to([128, 2, L])
                e.tensor_tensor(out=sv, in0=sv, in1=cv, op=ALU.add)
            else:
                sv = spv[:, :, 0:128]
                bv = band[:, 0:128].unsqueeze(1).broadcast_to([128, 2, 128])
                e.tensor_tensor(out=sv, in0=sv, in1=bv, op=ALU.add)
            sv2 = spv[:, :, 512:640]
            bv2 = band[:, 512:640].unsqueeze(1).broadcast_to([128, 2, 128])
            return e.tensor_tensor(out=sv2, in0=sv2, in1=bv2, op=ALU.add)
        em("dve", f, reads=[spb, R_band, R_combo], writes=spb)

    def emit_exp(n):
        par = n % 2
        p4 = n % 4
        sp, spb = sp_of(par)
        em("act", lambda e, sp=sp, par=par, p4=p4: e.activation(
            out=ebuf[par][:, 0:640], in_=sp[:, 0:640], func=AF.Exp,
            bias=expb, scale=SCALE, accum_out=S1[p4]),
           reads=spb + [R_sm], writes=[R_e[par], R_S1[p4]])
        em("act", lambda e, sp=sp, par=par, p4=p4: e.activation(
            out=ebuf[par][:, 640:1280], in_=sp[:, 640:1280], func=AF.Exp,
            bias=expb, scale=SCALE, accum_out=S2[p4]),
           reads=spb + [R_sm], writes=[R_e[par], R_S2[p4]])

    def emit_dump_e(n):
        par = n % 2
        em("syncq", lambda e, par=par: e.dma_start(out=dbg["dbg_e"], in_=ebuf[par]),
           reads=[R_e[par]], key="kdbg")

    def emit_stt(n):
        qb, h, j = insts[n]
        par = n % 2
        p4 = n % 4
        em("dve", lambda e, par=par, p4=p4: e.reciprocal(out=r2s[par], in_=S2[p4]),
           reads=[R_S2[p4]], writes=[R_r2[par]])
        em("dve", lambda e, par=par, p4=p4, h=h: e.tensor_scalar(
            out=ngc[par], in0=S1[p4], scalar1=r2s[par],
            scalar2=neglam[:, h:h + 1], op0=ALU.mult, op1=ALU.mult),
           reads=[R_S1[p4], R_r2[par], R_sm], writes=[R_ngc[par]])
        em("dve", lambda e, par=par: e.scalar_tensor_tensor(
            out=wp[par][:, 0:384], in0=ebuf[par][:, 640:1024], scalar=ngc[par],
            in1=ebuf[par][:, 0:384], op0=ALU.mult, op1=ALU.add),
           reads=[R_e[par], R_ngc[par]], writes=[R_wp[par]])
        em("pool", lambda e, par=par: e.tensor_scalar(
            out=pscr[par], in0=ebuf[par][:, 1024:1280], scalar1=ngc[par],
            scalar2=1.0, op0=ALU.mult, op1=ALU.mult),
           reads=[R_e[par], R_ngc[par]], writes=[R_pscr[par]])
        em("pool", lambda e, par=par: e.tensor_tensor(
            out=wp[par][:, 384:640], in0=pscr[par],
            in1=ebuf[par][:, 384:640], op=ALU.add),
           reads=[R_pscr[par], R_e[par]], writes=[R_wp[par]])

    def emit_relu(n):
        par = n % 2
        p4 = n % 4
        em("act", lambda e, par=par, p4=p4: e.activation(
            out=wbf[p4], in_=wp[par], func=AF.Relu, accum_out=gg[par]),
           reads=[R_wp[par]], writes=[R_wbf[p4], R_gg[par]])

    def emit_tds(n):
        par = n % 2
        p4 = n % 4
        em("dve", lambda e, par=par, p4=p4: e.tensor_scalar(
            out=tds[par], in0=S1[p4], scalar1=1e-6,
            scalar2=gg[par], op0=ALU.mult, op1=ALU.add),
           reads=[R_S1[p4], R_gg[par]], writes=[R_tds[par]])

    def emit_tds2(n):
        par = n % 2
        em("syncq", lambda e, par=par: e.dma_start(out=tds2[par], in_=tds[par]),
           reads=[R_tds[par]], writes=[R_tds[par]], key=f"ktd{par}")

    def emit_diffn(n):
        par = n % 2
        p4 = n % 4
        em("dve", lambda e, par=par: e.reciprocal(out=rv2[par], in_=tds2[par]),
           reads=[R_tds[par]], writes=[R_rv2[par]])
        em("pool", lambda e, par=par, p4=p4: e.tensor_scalar(
            out=dif[par], in0=wbf[p4], scalar1=rv2[par], scalar2=1.0,
            op0=ALU.mult, op1=ALU.mult),
           reads=[R_wbf[p4], R_rv2[par]], writes=[R_dif[par]])

    def emit_dump_small(n):
        par = n % 2

        def f(e, par=par):
            e.tensor_copy(out=dbg_sc_t[:, 0:1], in_=S1[par])
            e.tensor_copy(out=dbg_sc_t[:, 1:2], in_=S2[par])
            e.tensor_copy(out=dbg_sc_t[:, 2:3], in_=gg[par])
            e.tensor_copy(out=dbg_sc_t[:, 3:4], in_=r2s[par])
            e.tensor_copy(out=dbg_sc_t[:, 4:5], in_=ngc[par])
            e.tensor_copy(out=dbg_sc_t[:, 5:6], in_=rv2[par])
            return e.tensor_copy(out=dbg_sc_t[:, 6:7], in_=tds[par])
        em("dve", f, reads=[R_S1[par], R_S2[par], R_gg[par], R_r2[par], R_ngc[par],
                            R_rv2[par], R_tds[par]], writes=[R_sm])
        em("syncq", lambda e: e.dma_start(out=dbg["dbg_sc"], in_=dbg_sc_t), reads=[R_sm], key="kdbg")
        em("syncq", lambda e, par=par: e.dma_start(out=dbg["dbg_wp"], in_=wp[par]),
           reads=[R_wp[par]], key="kdbg")
        em("syncq", lambda e, par=par: e.dma_start(out=dbg["dbg_dif"], in_=dif[par]),
           reads=[R_dif[par]], key="kdbg")

    wtp = pm[:, 3584:3904].bitcast(BF16)
    wts3 = wts.rearrange("p (t c) -> p t c", c=512)

    def emit_transp(n):
        qb, h, j = insts[n]
        par = n % 2

        def f(e, par=par):
            last = None
            for k in range(5):
                last = e.transpose(wtp[:, 128 * k:128 * (k + 1)],
                                   dif[par][:, 128 * k:128 * (k + 1)], ident)
            return last
        em("pe", f, reads=[R_dif[par], R_sm], writes=[PB[7]])

    def emit_wtevac(n):
        qb, h, j = insts[n]
        em("dve", lambda e, j=j: e.tensor_copy(
            out=wts3[:, j:j + 5, 128 * j:128 * (j + 1)],
            in_=wtp.rearrange("p (t c) -> p t c", c=128)),
           reads=[PB[7]], writes=[R_wts])

    pav = pm[:, 3072:3584]

    def emit_av(n):
        qb, h, j = insts[n]
        assert j == 3
        kvh = h // 4

        def f(e, qb=qb, kvh=kvh):
            last = None
            for i in range(8):
                kc_abs = 4 * qb + i
                last = e.matmul(pav, vtm[:, 512 * kc_abs + 128 * kvh:512 * kc_abs + 128 * kvh + 128],
                                wts3[:, i, :], start=(i == 0), stop=(i == 7))
            return last
        em("pe", f, reads=[R_wts, R_v], writes=[PB[6]])

    def emit_avevac(n):
        qb, h, j = insts[n]
        em("act", lambda e, qb=qb, h=h: e.activation(
            out=attn[:, T * h + 512 * qb:T * h + 512 * qb + 512], in_=pav, func=AF.Identity),
           reads=[PB[6]], writes=[R_attn[h]])

    wo3 = [wo_buf[i].rearrange("p (kc f) -> p kc f", f=256) for i in range(2)]
    dma_wo = stream_w(wo_buf, R_wstr, Wo, 8, 256)
    dma_wo(0)
    dma_wo(1)

    for n in range(N + 14):
        if 0 <= n - 10 < N and insts[n - 10][2] == 3:
            emit_avevac(n - 10)
        if 0 <= n - 2 < N:
            emit_exp(n - 2)
            if debug and n - 2 == NDBG:
                emit_dump_e(n - 2)
        if n < N:
            emit_scores(n)
        if 0 <= n - 1 < N:
            emit_mask(n - 1)
        if 0 <= n - 3 < N:
            emit_stt(n - 3)
        if 0 <= n - 4 < N:
            emit_relu(n - 4)
        if 0 <= n - 5 < N:
            emit_tds(n - 5)
        if 0 <= n - 6 < N:
            emit_tds2(n - 6)
        if 0 <= n - 8 < N:
            emit_wtevac(n - 8)
            if insts[n - 8][2] == 3:
                emit_av(n - 8)
        if 0 <= n - 7 < N:
            emit_diffn(n - 7)
            emit_transp(n - 7)
            if debug and n - 7 == NDBG:
                emit_dump_small(n - 7)
        if 0 <= n - 13 < N and insts[n - 13][2] == 3:
            if debug and insts[n - 13] == (NDBG // 64, (NDBG // 4) % 16, 3):
                em("syncq", lambda e: e.dma_start(out=dbg["dbg_wts"], in_=wts),
                   reads=[R_wts], key="kdbg")

    if debug:
        em("syncq", lambda e: e.dma_start(out=dbg["dbg_attn"], in_=attn), reads=R_attn, key="kdbg")

    dma_w1 = stream_w(w1_buf, R_wstr, W1T, 16, 256)
    for cb in range(8):
        bi = cb % 2
        for sub in range(2):
            dc = 2 * cb + sub
            par = dc % 2
            pq = pm[:, 3072:4096] if par == 0 else pm[:, 0:1024]
            pqb = [PB[6], PB[7]] if par == 0 else [PB[0], PB[1]]

            def womm(e, bi=bi, sub=sub, pq=pq):
                last = None
                for kc in range(NKC):
                    for w in range(2):
                        last = e.matmul(pq[:, 512 * w:512 * (w + 1)],
                                        wo3[bi][:, kc, 128 * sub:128 * (sub + 1)],
                                        attn[:, T * kc + 512 * w:T * kc + 512 * w + 512],
                                        start=(kc == 0), stop=(kc == NKC - 1))
                return last
            em("pe", womm, reads=[R_wstr[bi]] + R_attn, writes=pqb)
            em("act", lambda e, dc=dc, pq=pq: e.activation(
                out=outac[:, T * dc:T * (dc + 1)], in_=pq, func=AF.Identity,
                bias=b3s[:, dc:dc + 1]),
               reads=pqb + [R_sm], writes=[R_out[dc]])
        if cb + 2 < 8:
            dma_wo(cb + 2)
        elif cb == 6:
            dma_w1(0)
        elif cb == 7:
            dma_w1(1)

    if debug:
        em("syncq", lambda e: e.dma_start(out=dbg["dbg_wo"], in_=outac), reads=R_out, key="kdbg")

    w13 = [w1_buf[i].rearrange("p (kc f) -> p kc f", f=256) for i in range(2)]
    for cb in range(16):
        bi = cb % 2
        for sub in range(2):
            fc = 2 * cb + sub
            par = fc % 2
            pq = pm[:, 3072:4096] if par == 0 else pm[:, 0:1024]
            pqb = [PB[6], PB[7]] if par == 0 else [PB[0], PB[1]]

            def w1mm(e, bi=bi, sub=sub, pq=pq):
                last = None
                for kc in range(NKC):
                    for w in range(2):
                        last = e.matmul(pq[:, 512 * w:512 * (w + 1)],
                                        w13[bi][:, kc, 128 * sub:128 * (sub + 1)],
                                        xnT3[:, kc, HALO + 512 * w:HALO + 512 * w + 512],
                                        start=(kc == 0), stop=(kc == NKC - 1))
                return last
            em("pe", w1mm, reads=[R_wstr[bi], R_xnT_all], writes=pqb)
            em("act", lambda e, fc=fc, pq=pq: e.activation(
                out=h1[:, T * fc:T * (fc + 1)], in_=pq, func=AF.Relu,
                bias=b1s[:, fc:fc + 1]),
               reads=pqb + [R_sm], writes=[R_h1[fc]])
        if cb + 2 < 16:
            dma_w1(cb + 2)

    if debug:
        em("syncq", lambda e: e.dma_start(out=dbg["dbg_h1"], in_=h1), reads=R_h1, key="kdbg")

    w23 = [w2_buf[i].rearrange("p (kc f) -> p kc f", f=128) for i in range(2)]
    w33 = [w3_buf[i].rearrange("p (kc f) -> p kc f", f=256) for i in range(2)]
    for blk in range(2):
        def dma_w2(ci):
            bi = ci % 2
            for half in range(2):
                em("gpq", lambda e, ci=ci, bi=bi, blk=blk, half=half: e.dma_start(
                    out=w23[bi][:, 16 * half:16 * (half + 1), :],
                    in_=W2T[2048 * half:2048 * (half + 1),
                            2048 * blk + 128 * ci:2048 * blk + 128 * (ci + 1)]
                    .rearrange("(kc p) f -> p kc f", p=128)),
                   writes=[R_w2[bi]], key=f"k2{bi}")
        dma_w2(0); dma_w2(1)
        for fc2 in range(16):
            bi = fc2 % 2
            par = fc2 % 2
            pq = pm[:, 3072:4096] if par == 0 else pm[:, 0:1024]
            pqb = [PB[6], PB[7]] if par == 0 else [PB[0], PB[1]]

            def w2mm(e, bi=bi, pq=pq):
                last = None
                for kc in range(32):
                    for w in range(2):
                        last = e.matmul(pq[:, 512 * w:512 * (w + 1)],
                                        w23[bi][:, kc, :],
                                        h1[:, T * kc + 512 * w:T * kc + 512 * w + 512],
                                        start=(kc == 0), stop=(kc == 31))
                return last
            em("pe", w2mm, reads=[R_w2[bi]] + R_h1, writes=pqb)
            em("act", lambda e, fc2=fc2, blk=blk, pq=pq: e.activation(
                out=h2[:, T * fc2:T * (fc2 + 1)], in_=pq, func=AF.Relu,
                bias=b2s[:, 16 * blk + fc2:16 * blk + fc2 + 1]),
               reads=pqb + [R_sm], writes=[R_h2[fc2]])
            if fc2 + 2 < 16:
                dma_w2(fc2 + 2)

        def dma_w3(ci):
            bi = ci % 2
            em("gpq", lambda e, ci=ci, bi=bi, blk=blk: e.dma_start(
                out=w33[bi],
                in_=W3T[2048 * blk:2048 * (blk + 1), 256 * ci:256 * (ci + 1)]
                .rearrange("(kc p) f -> p kc f", p=128)),
               writes=[R_w3[bi]], key=f"k3{bi}")
        dma_w3(0); dma_w3(1)
        for cb in range(8):
            bi = cb % 2
            for sub in range(2):
                dc = 2 * cb + sub
                par = dc % 2
                pq = pm[:, 1024:2048] if par == 0 else pm[:, 2048:3072]
                pqb = [PB[2], PB[3]] if par == 0 else [PB[4], PB[5]]

                def w3mm(e, bi=bi, sub=sub, pq=pq):
                    last = None
                    for kc in range(16):
                        for w in range(2):
                            last = e.matmul(pq[:, 512 * w:512 * (w + 1)],
                                            w33[bi][:, kc, 128 * sub:128 * (sub + 1)],
                                            h2[:, T * kc + 512 * w:T * kc + 512 * w + 512],
                                            start=(kc == 0), stop=(kc == 15))
                    return last
                em("pe", w3mm, reads=[R_w3[bi]] + R_h2, writes=pqb)
                em("dve", lambda e, dc=dc, pq=pq: e.tensor_tensor(
                    out=outac[:, T * dc:T * (dc + 1)], in0=pq,
                    in1=outac[:, T * dc:T * (dc + 1)], op=ALU.add),
                   reads=pqb + [R_out[dc]], writes=[R_out[dc]])
            if cb + 2 < 8:
                dma_w3(cb + 2)

    for tt in range(8):
        par = tt % 2
        xt = xf[par]
        rxt = R_wstr[par]
        em("syncq", lambda e, tt=tt, xt=xt: e.dma_start(
            out=xt, in_=xh[HALO + 128 * tt:HALO + 128 * (tt + 1), :]),
           writes=[rxt], key=f"kx{tt % 2}")
        pf = pm[:, 0:2048] if par == 0 else pm[:, 2048:4096]
        pfb = [PB[0], PB[1], PB[2], PB[3]] if par == 0 else [PB[4], PB[5], PB[6], PB[7]]

        def ftr(e, tt=tt, pf=pf):
            last = None
            for dc in range(NKC):
                last = e.transpose(pf[:, 128 * dc:128 * (dc + 1)],
                                   outac[:, T * dc + 128 * tt:T * dc + 128 * (tt + 1)],
                                   ident32)
            return last
        em("pe", ftr, reads=R_out + [R_sm], writes=pfb)
        em("dve", lambda e, par=par, pf=pf, xt=xt: e.tensor_tensor(
            out=outs[par], in0=pf, in1=xt, op=ALU.add),
           reads=pfb + [rxt], writes=[R_outs[par]])
        em("syncq", lambda e, tt=tt, par=par: e.dma_start(
            out=y[128 * tt:128 * (tt + 1), :], in_=outs[par]),
           reads=[R_outs[par]], key=f"kyo{par}")

    s.sem("kyo0"); s.sem("kyo1")
    fin0, fin1 = s.cnt["kyo0"], s.cnt["kyo1"]

    with nc.Block() as block:
        @block.sync
        def _(e):
            s.replay(("syncq",), e)
            e.wait_ge(s.sems["kyo0"], fin0)
            e.wait_ge(s.sems["kyo1"], fin1)

        @block.vector
        def _(e):
            s.replay(("dve",), e)

        @block.scalar
        def _(e):
            s.replay(("act",), e)

        @block.tensor
        def _(e):
            s.replay(("pe",), e)

        @block.gpsimd
        def _(e):
            s.replay(("pool", "gpq"), e)

    return nc



def host_inputs(x_full, Wq, Wk, Wv, Wo, lam, W1, b1, W2, b2, W3, b3):
    import ml_dtypes
    f32 = np.float32
    xr = np.ascontiguousarray(x_full, dtype=f32).reshape(NCORES * T, D)
    W1T = np.ascontiguousarray(np.asarray(W1, f32).T)
    W2T = np.ascontiguousarray(np.asarray(W2, f32).T)
    W3T = np.ascontiguousarray(np.asarray(W3, f32).T)
    b1t = np.ascontiguousarray(np.asarray(b1, f32).reshape(32, 128).T)
    b2t = np.ascontiguousarray(np.asarray(b2, f32).reshape(32, 128).T)
    b3t = np.ascontiguousarray(np.asarray(b3, f32).reshape(16, 128).T)
    lam_b = np.ascontiguousarray(np.broadcast_to(np.asarray(lam, f32)[None, :], (128, 16)))

    r = np.arange(128)[:, None]
    kk = np.arange(640)[None, :]
    bandv = np.where((kk - r >= 1) & (kk - r <= WIN), 0.0, NEG).astype(f32)
    band = np.concatenate([bandv, bandv], axis=1).astype(ml_dtypes.bfloat16)

    maps = []
    for c in range(NCORES):
        sq = c % 4
        t0 = sq * T
        own = xr[c * T:(c + 1) * T]
        if sq == 0:
            halo = np.zeros((HALO, D), f32)
        else:
            halo = xr[c * T - HALO:c * T]
        xh = np.concatenate([halo, own], axis=0)

        pos = (np.arange(t0 - HALO, t0 + T, dtype=f32))[None, :]
        invf = (f32(1.0) / (f32(10000.0) ** (np.arange(0, HD, 2, dtype=f32) / f32(HD))))[:, None]
        th = invf * pos
        cosv = np.cos(np.concatenate([th, th], 0)).astype(f32)
        sinv = np.sin(np.concatenate([th, th], 0)).astype(f32)
        sinm = sinv.copy()
        sinm[0:64] *= -1.0
        if sq == 0:
            cosv[:, 0:HALO] = 1.0
            sinm[:, 0:HALO] = 0.0

        combov = np.zeros((128, 512), f32)
        if sq == 0:
            combov[:, :] = NEG
        else:
            combov[:, 0:128] = bandv[:, 0:128]
        maps.append(dict(
            xh=xh, Wq=Wq, Wk=Wk, Wv=Wv, Wo=Wo, W1T=W1T, W2T=W2T, W3T=W3T,
            b1t=b1t, b2t=b2t, b3t=b3t, lam_bc=lam_b,
            ropecos=np.ascontiguousarray(cosv), ropesinm=np.ascontiguousarray(sinm),
            band=band, combo=combov.astype(ml_dtypes.bfloat16),
        ))
    return maps


def kernel(L0_attn_x, L0_attn_Wq, L0_attn_Wk, L0_attn_Wv, L0_attn_Wo, L0_attn_lam,
           L0_ffn_W1, L0_ffn_b1, L0_ffn_W2, L0_ffn_b2, L0_ffn_W3, L0_ffn_b3):
    f32 = np.float32
    maps = host_inputs(np.asarray(L0_attn_x, f32), np.asarray(L0_attn_Wq, f32),
                       np.asarray(L0_attn_Wk, f32), np.asarray(L0_attn_Wv, f32),
                       np.asarray(L0_attn_Wo, f32), np.asarray(L0_attn_lam, f32),
                       np.asarray(L0_ffn_W1, f32), np.asarray(L0_ffn_b1, f32),
                       np.asarray(L0_ffn_W2, f32), np.asarray(L0_ffn_b2, f32),
                       np.asarray(L0_ffn_W3, f32), np.asarray(L0_ffn_b3, f32))
    nc = build()
    res = run_bass_kernel_spmd(nc, maps, core_ids=list(range(NCORES)))
    out = np.concatenate([r["y"] for r in res.results], axis=0)
    return out.reshape(2, 4096, D)


if __name__ == "__main__":
    nc = build()
    print("built ok")


# version 31
# speedup vs baseline: 1.0284x; 1.0284x over previous
import numpy as np
import concourse.bass as bass
import concourse.mybir as mybir
from concourse.bass_utils import run_bass_kernel_spmd

F32 = mybir.dt.float32
BF16 = mybir.dt.bfloat16
AF = mybir.ActivationFunctionType
ALU = mybir.AluOpType

NCORES = 8
T = 1024
HALO = 512
A = T + HALO
D = 2048
NH, NKV, HD, SUB, WIN = 16, 4, 128, 64, 512
DFF = 4096
NTT = A // 128
NKC = D // 128
NEG = -1e30
EXPB = -16.0
SCALE = 0.125

ARENA = 203 * 1024
SLOT = 512

OFFS = {
    "xnT":   (0,      NKC * A * 2),
    "wstr":  (49152,  16384),
    "qfm":   (65536,  NH * T * 2),
    "kfm":   (98304,  NKV * A * 2),
    "vtm":   (110592, NTT * 512 * 2),
    "scr":   (122880, 18432),
    "rope":  (141312, 2 * A * 4),
    "e":     (141312, 2 * 1280 * 4),
    "attn":  (153600, NH * T * 2),
    "outac": (65536,  NKC * T * 4),
    "h1":    (131072, 32 * T * 2),
    "h2":    (0,      16 * T * 2),
    "w2s":   (32768,  16384),
    "w3s":   (49152,  16384),
    "outs":  (131072, 16384),
    "band":  (196608, 1280 * 2),
    "combo": (199168, 512 * 2),
    "smalls": (200192, 2560),
}
SCR = OFFS["scr"][0]
OFF_QTMP = [SCR, SCR + 6144]
OFF_RTMP = SCR + 12288
OFF_WTS = SCR
OFF_WP = [SCR + 8192, SCR + 8192 + 2560]
OFF_WBF = [SCR + 13312, SCR + 13312 + 1280]
OFF_DIF = [SCR + 15872, SCR + 15872 + 1280]

DMA_ENGS = ("syncq", "gpq")


class R:
    __slots__ = ("w", "rs")

    def __init__(self):
        self.w = None
        self.rs = []


class Sched:

    def __init__(self, nc):
        self.nc = nc
        self.sems = {}
        self.cnt = {}
        self.ops = []
        self.slots = [R() for _ in range((ARENA // SLOT) + 2)]
        self.pbank = [R() for _ in range(8)]

    def sem(self, key):
        if key not in self.sems:
            self.sems[key] = self.nc.alloc_semaphore(key + "_sem")
            self.cnt[key] = 0
        return self.sems[key]

    def srange(self, off, nbytes):
        return self.slots[off // SLOT: (off + nbytes + SLOT - 1) // SLOT]

    def em(self, eng, fn, reads=(), writes=(), key=None):
        key = key or eng
        self.sem(key)
        waits = {}

        def need(dep):
            dkey, dcnt = dep
            if dkey == key == "pe":
                return
            if waits.get(dkey, -1) < dcnt:
                waits[dkey] = dcnt
        rset = []
        for r in reads:
            rset.extend(r if isinstance(r, (list, tuple)) else [r])
        wset = []
        for r in writes:
            wset.extend(r if isinstance(r, (list, tuple)) else [r])
        for r in rset:
            if r.w:
                need(r.w)
        for r in wset:
            if r.w:
                need(r.w)
            for rr in r.rs:
                need(rr)
        inc = 16 if eng in DMA_ENGS else 1
        self.cnt[key] += inc
        my = (key, self.cnt[key])
        for r in rset:
            r.rs.append(my)
        for r in wset:
            r.w = my
            r.rs = []
        self.ops.append((eng, key, fn, waits, inc))

    def replay(self, engkeys, e):
        waited = {}
        for eng, key, fn, waits, inc in self.ops:
            if eng not in engkeys:
                continue
            for sname, val in waits.items():
                if waited.get(sname, -1) < val:
                    e.wait_ge(self.sems[sname], val)
                    waited[sname] = val
            inst = fn(e)
            inst.then_inc(self.sems[key], inc)


def build(debug=False):
    nc = bass.Bass("TRN2", target_bir_lowering=False, debug=False)

    dram = {}
    def din(name, shape, dtype=F32):
        dram[name] = nc.dram_tensor(name, shape, dtype, kind="ExternalInput").ap()
        return dram[name]

    xh = din("xh", [A, D])
    Wq = din("Wq", [D, NH * HD]); Wk = din("Wk", [D, NKV * HD]); Wv = din("Wv", [D, NKV * HD])
    Wo = din("Wo", [NH * HD, D])
    W1T = din("W1T", [D, DFF]); W2T = din("W2T", [DFF, DFF]); W3T = din("W3T", [DFF, D])
    b1t = din("b1t", [128, 32]); b2t = din("b2t", [128, 32]); b3t = din("b3t", [128, 16])
    lam_bc = din("lam_bc", [128, 16])
    ropecos = din("ropecos", [128, A]); ropesinm = din("ropesinm", [128, A])
    band_d = din("band", [128, 1280], BF16)
    combo_d = din("combo", [128, 512], BF16)
    y = nc.dram_tensor("y", [T, D], F32, kind="ExternalOutput").ap()
    dbg = {}
    if debug:
        for nm, shape, dt in [
            ("dbg_xnT", [128, NKC * A], BF16), ("dbg_q", [128, NH * T], BF16),
            ("dbg_k", [128, NKV * A], BF16), ("dbg_v", [128, NTT * 512], BF16),
            ("dbg_attn", [128, NH * T], BF16), ("dbg_wo", [128, NKC * T], F32),
            ("dbg_h1", [128, 32 * T], BF16),
            ("dbg_sp", [128, 1280], F32), ("dbg_e", [128, 1280], F32),
            ("dbg_sc", [128, 8], F32), ("dbg_wp", [128, 640], F32),
            ("dbg_dif", [128, 640], BF16), ("dbg_wts", [128, 4096], BF16),
        ]:
            dbg[nm] = nc.dram_tensor(nm, shape, dt, kind="ExternalOutput").ap()

    arena = nc.alloc_sbuf_tensor("arena", [128, ARENA], mybir.dt.uint8)
    base = None
    for alloc in nc.m.functions[0].allocations:
        try:
            ml = alloc.memorylocations[0]
            if ml.name == "arena":
                base = ml.addr
        except Exception:
            pass
    assert base is not None

    def sb(name, shape, dtype, off):
        t = nc.alloc_sbuf_tensor_at(name, shape, dtype, offset=base + off)
        return t.ap()

    xnT = sb("xnT", [128, NKC * A], BF16, OFFS["xnT"][0])
    XF_OFFS = [OFFS["wstr"][0], OFFS["wstr"][0] + 8192,
               OFFS["qfm"][0], OFFS["qfm"][0] + 8192, OFFS["qfm"][0] + 16384]
    xf = [sb(f"xf{i}", [128, D], F32, XF_OFFS[i]) for i in range(5)]
    xnb4 = [sb(f"xnb{i}", [128, D], BF16, OFFS["qfm"][0] + 24576 + 4096 * i) for i in range(4)]
    nmr = [sb(f"nmr{i}", [128, 1], F32, OFFS["smalls"][0] + 2496 + 32 * i) for i in range(2)]
    wkq_buf = [sb(f"wkq{i}", [128, NKC * 256], BF16, OFFS["wstr"][0] + i * 8192) for i in range(2)]
    wq_buf = [sb(f"wqb{i}", [128, NKC * 256], BF16, OFFS["kfm"][0] + 8192 + i * 8192) for i in range(2)]
    wv_buf = sb("wvb", [128, NKC * 512], BF16, OFFS["wstr"][0])
    wo_buf = [sb(f"wob{i}", [128, NKC * 256], BF16, OFFS["wstr"][0] + i * 8192) for i in range(2)]
    w1_buf = [sb(f"w1b{i}", [128, NKC * 256], BF16, OFFS["wstr"][0] + i * 8192) for i in range(2)]
    qfm = sb("qfm", [128, NH * T], BF16, OFFS["qfm"][0])
    kfm = sb("kfm", [128, NKV * A], BF16, OFFS["kfm"][0])
    vtm = sb("vtm", [128, NTT * 512], BF16, OFFS["vtm"][0])
    qtmp = [sb(f"qtmp{i}", [128, A], F32, OFF_QTMP[i]) for i in range(2)]
    rtmp = sb("rtmp", [128, A], F32, OFF_RTMP)
    wts = sb("wts", [128, 8 * 512], BF16, OFF_WTS)
    wp = [sb(f"wp{i}", [128, 640], F32, OFF_WP[i]) for i in range(2)]
    WBF_OFFS = [OFF_WBF[0], OFF_WBF[1], OFFS["h1"][0] + 55296, OFFS["h1"][0] + 55296 + 1280]
    wbf = [sb(f"wbf{i}", [128, 640], BF16, WBF_OFFS[i]) for i in range(4)]
    PSCR_OFFS = [OFFS["h1"][0] + 57856, OFFS["h1"][0] + 59136]
    pscr = [sb(f"pscr{i}", [128, 256], F32, PSCR_OFFS[i]) for i in range(2)]
    dif = [sb(f"dif{i}", [128, 640], BF16, OFF_DIF[i]) for i in range(2)]
    ropec = sb("ropec", [128, A], F32, OFFS["rope"][0])
    ropes = sb("ropes", [128, A], F32, OFFS["rope"][0] + A * 4)
    ebuf = [sb(f"ebuf{i}", [128, 1280], F32, OFFS["e"][0] + i * 5120) for i in range(2)]
    attn = sb("attn", [128, NH * T], BF16, OFFS["attn"][0])
    outac = sb("outac", [128, NKC * T], F32, OFFS["outac"][0])
    h1 = sb("h1", [128, 32 * T], BF16, OFFS["h1"][0])
    h2 = sb("h2", [128, 16 * T], BF16, OFFS["h2"][0])
    w2_buf = [sb(f"w2b{i}", [128, 32 * 128], BF16, OFFS["w2s"][0] + i * 8192) for i in range(2)]
    w3_buf = [sb(f"w3b{i}", [128, 16 * 256], BF16, OFFS["w3s"][0] + i * 8192) for i in range(2)]
    outs = [sb(f"outs{i}", [128, D], F32, OFFS["outs"][0] + i * 8192) for i in range(2)]
    band = sb("bandt", [128, 1280], BF16, OFFS["band"][0])
    combo = sb("combot", [128, 512], BF16, OFFS["combo"][0])
    SM = OFFS["smalls"][0]
    b1s = sb("b1s", [128, 32], F32, SM + 0)
    b2s = sb("b2s", [128, 32], F32, SM + 128)
    b3s = sb("b3s", [128, 16], F32, SM + 256)
    neglam = sb("neglam", [128, 16], F32, SM + 320)
    lam_s = sb("lam_s", [128, 16], F32, SM + 384)
    ident = sb("ident", [128, 128], BF16, SM + 448)
    ident32 = sb("ident32", [128, 128], F32, SM + 704)
    bnst = [sb(f"bnst{i}", [128, 24], F32, SM + 1216 + 96 * i) for i in range(2)]
    mv = [sb(f"mv{i}", [128, 2], F32, SM + 2240 + 32 * i) for i in range(2)]
    ln_s = [sb(f"ln_s{i}", [128, 6], F32, SM + 2304 + 32 * i) for i in range(2)]
    S1 = [sb(f"S1_{i}", [128, 1], F32, SM + 1376 + 32 * i) for i in range(4)]
    S2 = [sb(f"S2_{i}", [128, 1], F32, SM + 1504 + 32 * i) for i in range(4)]
    gg = [sb(f"gg_{i}", [128, 1], F32, SM + 1632 + 32 * i) for i in range(2)]
    r2s = [sb(f"r2_{i}", [128, 1], F32, SM + 1696 + 32 * i) for i in range(2)]
    ngc = [sb(f"ngc_{i}", [128, 1], F32, SM + 1760 + 32 * i) for i in range(2)]
    rv2 = [sb(f"rv2_{i}", [128, 1], F32, SM + 1824 + 32 * i) for i in range(2)]
    tds = [sb(f"tds_{i}", [128, 1], F32, SM + 1888 + 32 * i) for i in range(2)]
    expb = sb("expb", [128, 1], F32, SM + 2016)
    eps_t = sb("eps_t", [128, 1], F32, SM + 2048)
    bnst2 = [sb(f"bnst2{i}", [128, 24], F32, SM + 2080 + 0 * i) for i in range(1)]
    bnst2b = sb("bnst2b", [128, 24], F32, SM + 2368)
    bnst2 = [bnst2[0], bnst2b]
    tds2 = [sb(f"tds2_{i}", [128, 1], F32, SM + 1952 + 32 * i) for i in range(2)]

    pm_h = nc.alloc_psum_tensor("pm", [128, 4096], F32)
    pm = pm_h.ap()

    s = Sched(nc)

    res_cache = {}
    def rs(name, off, nbytes):
        key = (off, nbytes)
        if key not in res_cache:
            res_cache[key] = s.srange(off, nbytes)
        return res_cache[key]

    R_xnT_all = rs("xnTall", OFFS["xnT"][0], OFFS["xnT"][1])
    R_xnT_own = R()
    R_wstr = [rs("wstr", OFFS["wstr"][0] + i * 8192, 8192) for i in range(2)]
    R_wq = [rs("wq", OFFS["kfm"][0] + 8192 + i * 8192, 8192) for i in range(2)]
    R_xf = [rs("xf", XF_OFFS[i], 8192) for i in range(5)]
    R_xnb = [rs("xnb", OFFS["qfm"][0] + 24576 + 4096 * i, 4096) for i in range(4)]
    R_q = [rs("q", OFFS["qfm"][0] + h * T * 2, T * 2) for h in range(NH)]
    R_k = rs("k", OFFS["kfm"][0], OFFS["kfm"][1])
    R_v = rs("v", OFFS["vtm"][0], OFFS["vtm"][1])
    R_qtmp = [rs("qtmp", OFF_QTMP[i], 6144) for i in range(2)]
    R_rtmp = rs("rtmp", OFF_RTMP, 6144)
    R_wts = rs("wts", OFF_WTS, 8192)
    R_wp = [rs("wp", OFF_WP[i], 2560) for i in range(2)]
    R_wbf = [rs("wbf", WBF_OFFS[i], 1280) for i in range(4)]
    R_pscr = [rs("pscr", PSCR_OFFS[i], 1024) for i in range(2)]
    R_dif = [rs("dif", OFF_DIF[i], 1280) for i in range(2)]
    R_rope = rs("rope", OFFS["rope"][0], OFFS["rope"][1])
    R_e = [rs("e", OFFS["e"][0] + i * 5120, 5120) for i in range(2)]
    R_attn = [rs("attn", OFFS["attn"][0] + h * T * 2, T * 2) for h in range(NH)]
    R_out = [rs("out", OFFS["outac"][0] + c * T * 4, T * 4) for c in range(NKC)]
    R_h1 = [rs("h1", OFFS["h1"][0] + c * T * 2, T * 2) for c in range(32)]
    R_h2 = [rs("h2", OFFS["h2"][0] + c * T * 2, T * 2) for c in range(16)]
    R_w2 = [rs("w2s", OFFS["w2s"][0] + i * 8192, 8192) for i in range(2)]
    R_w3 = [rs("w3s", OFFS["w3s"][0] + i * 8192, 8192) for i in range(2)]
    R_outs = [rs("outs", OFFS["outs"][0] + i * 8192, 8192) for i in range(2)]
    R_band = rs("band", OFFS["band"][0], OFFS["band"][1])
    R_combo = rs("combo", OFFS["combo"][0], OFFS["combo"][1])
    R_sm = rs("smalls", SM, 1216)
    R_lnA = [R() for _ in range(2)]
    R_S1 = [R() for _ in range(4)]
    R_S2 = [R() for _ in range(4)]
    R_gg = [R() for _ in range(2)]
    R_r2 = [R() for _ in range(2)]
    R_ngc = [R() for _ in range(2)]
    R_rv2 = [R() for _ in range(2)]
    R_tds = [R() for _ in range(2)]

    PB = s.pbank

    em = s.em

    em("syncq", lambda e: e.dma_start(out=band, in_=band_d), writes=[R_band], key="kc_band")
    em("syncq", lambda e: e.dma_start(out=combo, in_=combo_d), writes=[R_combo], key="kc_combo")
    em("syncq", lambda e: e.dma_start(out=ropec, in_=ropecos), writes=[R_rope], key="kc_ropec")
    em("syncq", lambda e: e.dma_start(out=ropes, in_=ropesinm), writes=[R_rope], key="kc_ropes")
    em("syncq", lambda e: e.dma_start(out=b1s, in_=b1t), writes=[R_sm], key="kc_b1")
    em("syncq", lambda e: e.dma_start(out=b2s, in_=b2t), writes=[R_sm], key="kc_b2")
    em("syncq", lambda e: e.dma_start(out=b3s, in_=b3t), writes=[R_sm], key="kc_b3")
    em("syncq", lambda e: e.dma_start(out=lam_s, in_=lam_bc), writes=[R_sm], key="kc_lam")
    em("dve", lambda e: e.tensor_scalar(out=neglam, in0=lam_s, scalar1=-1.0,
                                        scalar2=None, op0=ALU.mult),
       reads=[R_sm], writes=[R_sm])
    em("pool", lambda e: e.memset(ident, 1.0), writes=[R_sm])
    em("pool", lambda e: e.affine_select(out=ident, in_=ident, pattern=[[-1, 128]],
                                         base=0, channel_multiplier=1,
                                         compare_op=ALU.is_equal, fill=0.0),
       reads=[R_sm], writes=[R_sm])
    em("dve", lambda e: e.tensor_copy(out=ident32, in_=ident), reads=[R_sm], writes=[R_sm])
    em("dve", lambda e: e.memset(expb, EXPB), writes=[R_sm])
    em("dve", lambda e: e.memset(eps_t, 1e-5), writes=[R_sm])

    paT2 = [pm[:, 3072:4096].bitcast(BF16), pm[:, 0:1024].bitcast(BF16)]
    paTB = [[PB[6], PB[7]], [PB[0], PB[1]]]
    xnT3 = xnT.rearrange("p (c t) -> p c t", t=A)

    def emA_load(tt, pos):
        p5 = pos % 5
        em("syncq", lambda e, tt=tt, p5=p5: e.dma_start(
            out=xf[p5], in_=xh[128 * tt:128 * (tt + 1), :]),
           writes=[R_xf[p5]], key=f"kx{p5}")

    def emA_stats(tt, pos):
        p5, p = pos % 5, tt % 2
        for kseg in range(4):
            em("dve", lambda e, p=p, p5=p5, kseg=kseg: e.bn_stats(
                out=bnst[p][:, 6 * kseg:6 * kseg + 6], in_=xf[p5][:, 512 * kseg:512 * (kseg + 1)]),
               reads=[R_xf[p5]], writes=[R_lnA[p]])

    def emA_bounce(tt):
        p = tt % 2
        em("act", lambda e, p=p: e.activation(out=bnst2[p], in_=bnst[p], func=AF.Identity),
           reads=[R_lnA[p]], writes=[R_lnA[p]])

    def emA_aggr(tt):
        p = tt % 2
        em("dve", lambda e, p=p: e.bn_aggr(out=mv[p], in_=bnst2[p]),
           reads=[R_lnA[p]], writes=[R_lnA[p]])

    def emA_sqrt(tt):
        p = tt % 2
        em("act", lambda e, p=p: e.activation(out=ln_s[p][:, 1:2], in_=mv[p][:, 1:2], func=AF.Sqrt,
                                              bias=eps_t, scale=1.0),
           reads=[R_lnA[p], R_sm], writes=[R_lnA[p]])

    def emA_norm(tt, pos):
        p5, p4, p = pos % 5, tt % 4, tt % 2
        em("dve", lambda e, p=p: e.reciprocal(out=ln_s[p][:, 2:3], in_=ln_s[p][:, 1:2]),
           reads=[R_lnA[p]], writes=[R_lnA[p]])
        em("dve", lambda e, p=p: e.tensor_scalar(
            out=nmr[p], in0=mv[p][:, 0:1], scalar1=ln_s[p][:, 2:3], scalar2=-1.0,
            op0=ALU.mult, op1=ALU.mult),
           reads=[R_lnA[p]], writes=[R_lnA[p]])
        em("pool", lambda e, p=p, p4=p4, p5=p5: e.tensor_scalar(
            out=xnb4[p4], in0=xf[p5], scalar1=ln_s[p][:, 2:3], scalar2=nmr[p],
            op0=ALU.mult, op1=ALU.add),
           reads=[R_xf[p5], R_lnA[p]], writes=[R_xnb[p4]])

    def emA_transp(tt):
        p4, p = tt % 4, tt % 2

        def transposes(e, p=p, p4=p4):
            last = None
            for c in range(NKC):
                last = e.transpose(paT2[p][:, 128 * c:128 * (c + 1)],
                                   xnb4[p4][:, 128 * c:128 * (c + 1)], ident)
            return last
        em("pe", transposes, reads=[R_xnb[p4], R_sm], writes=paTB[p])

    def emA_evac(tt):
        p = tt % 2
        extra = [R_xnT_own] if tt == NTT - 1 else []
        em("act", lambda e, tt=tt, p=p: e.activation(
            out=xnT3[:, :, 128 * tt:128 * (tt + 1)],
            in_=paT2[p].rearrange("p (c t) -> p c t", t=128), func=AF.Identity),
           reads=paTB[p], writes=[R_xnT_all] + extra)

    def stream_w(buf_aps, buf_res, dram_ap, chunks, cols, keypfx="kw"):
        def dma(ci):
            bi = ci % 2
            em("gpq", lambda e, ci=ci, bi=bi: e.dma_start(
                out=buf_aps[bi].rearrange("p (kc f) -> p kc f", f=cols),
                in_=dram_ap[:, cols * ci:cols * (ci + 1)].rearrange("(kc p) f -> p kc f", p=128)),
               writes=[buf_res[bi]], key=f"{keypfx}{bi}")
        return dma

    wq3 = [wq_buf[i].rearrange("p (kc f) -> p kc f", f=256) for i in range(2)]
    dma_wq = stream_w(wq_buf, R_wq, Wq, 8, 256, keypfx="kwq")
    dma_wq(0); dma_wq(1)

    def emit_qmm(h):
        bi = (h // 2) % 2
        sub = h % 2
        pq = pm[:, 1024:2048] if sub == 0 else pm[:, 2048:3072]
        pqb = [PB[2], PB[3]] if sub == 0 else [PB[4], PB[5]]

        def qmm(e, bi=bi, sub=sub, pq=pq):
            last = None
            for kc in range(NKC):
                for w in range(2):
                    last = e.matmul(pq[:, 512 * w:512 * (w + 1)],
                                    wq3[bi][:, kc, 128 * sub:128 * (sub + 1)],
                                    xnT3[:, kc, HALO + 512 * w:HALO + 512 * (w + 1)],
                                    start=(kc == 0), stop=(kc == NKC - 1))
            return last
        em("pe", qmm, reads=[R_wq[bi], R_xnT_own], writes=pqb)

    A_ORDER = list(range(4, NTT)) + list(range(4))
    for it in range(NTT + 7):
        if 0 <= it - 2 < NTT:
            emA_stats(A_ORDER[it - 2], it - 2)
        if 0 <= it - 3 < NTT:
            emA_bounce(A_ORDER[it - 3])
        if 0 <= it - 4 < NTT:
            emA_aggr(A_ORDER[it - 4])
            emA_sqrt(A_ORDER[it - 4])
        if 0 <= it - 5 < NTT:
            emA_norm(A_ORDER[it - 5], it - 5)
        if 0 <= it - 7 < NTT:
            emA_evac(A_ORDER[it - 7])
        if it == 14:
            emit_qmm(0); emit_qmm(1)
        if 0 <= it - 6 < NTT:
            emA_transp(A_ORDER[it - 6])
        if it < NTT:
            emA_load(A_ORDER[it], it)

    if debug:
        em("syncq", lambda e: e.dma_start(out=dbg["dbg_xnT"], in_=xnT), reads=[R_xnT_all], key="kdbg")

    for qb in range(8):
        for sub in range(2):
            h = 2 * qb + sub
            par = h % 2
            pq = pm[:, 1024:2048] if par == 0 else pm[:, 2048:3072]
            pqb = [PB[2], PB[3]] if par == 0 else [PB[4], PB[5]]
            if h >= 2:
                emit_qmm(h)
            em("act", lambda e, pq=pq, par=par: e.activation(
                out=qtmp[par][:, 0:T], in_=pq, func=AF.Identity),
               reads=pqb, writes=[R_qtmp[par]])

            def qshift(e, pq=pq):
                e.activation(out=rtmp[0:64, 0:T], in_=pq[64:128, :], func=AF.Identity)
                return e.activation(out=rtmp[64:128, 0:T], in_=pq[0:64, :], func=AF.Identity)
            em("act", qshift, reads=pqb, writes=[R_rtmp])

            def qrope1(e, par=par):
                qt = qtmp[par]
                e.tensor_tensor(out=rtmp[:, 0:T], in0=rtmp[:, 0:T],
                                in1=ropes[:, HALO:A], op=ALU.mult)
                return e.tensor_tensor(out=qt[:, 0:T], in0=qt[:, 0:T],
                                       in1=ropec[:, HALO:A], op=ALU.mult)
            em("dve", qrope1, reads=[R_qtmp[par], R_rtmp, R_rope], writes=[R_qtmp[par], R_rtmp])
            em("dve", lambda e, h=h, par=par: e.tensor_tensor(
                out=qfm[:, T * h:T * (h + 1)], in0=qtmp[par][:, 0:T],
                in1=rtmp[:, 0:T], op=ALU.add),
               reads=[R_qtmp[par], R_rtmp], writes=[R_q[h]])
        if qb + 2 < 8:
            dma_wq(qb + 2)

    kq3 = [wkq_buf[i].rearrange("p (kc f) -> p kc f", f=256) for i in range(2)]
    dma_wk = stream_w(wkq_buf, R_wstr, Wk, 2, 256)
    dma_wk(0); dma_wk(1)
    for kb in range(2):
        bi = kb % 2
        for sub in range(2):
            kvh = 2 * kb + sub
            pq = pm[:, 0:1536] if (kvh % 2 == 0) else pm[:, 1536:3072]
            pqb = [PB[0], PB[1], PB[2]] if (kvh % 2 == 0) else [PB[3], PB[4], PB[5]]

            def kmm(e, bi=bi, sub=sub, pq=pq):
                last = None
                for kc in range(NKC):
                    for w in range(3):
                        last = e.matmul(pq[:, 512 * w:512 * (w + 1)],
                                        kq3[bi][:, kc, 128 * sub:128 * (sub + 1)],
                                        xnT3[:, kc, 512 * w:512 * (w + 1)],
                                        start=(kc == 0), stop=(kc == NKC - 1))
                return last
            em("pe", kmm, reads=[R_wstr[bi], R_xnT_all], writes=pqb)
            em("act", lambda e, pq=pq: e.activation(out=qtmp[0][:, 0:A], in_=pq, func=AF.Identity),
               reads=pqb, writes=[R_qtmp[0]])

            def kshift(e, pq=pq):
                e.activation(out=rtmp[0:64, 0:A], in_=pq[64:128, :], func=AF.Identity)
                return e.activation(out=rtmp[64:128, 0:A], in_=pq[0:64, :], func=AF.Identity)
            em("act", kshift, reads=pqb, writes=[R_rtmp])

            def krope1(e):
                qt = qtmp[0]
                e.tensor_tensor(out=rtmp[:, 0:A], in0=rtmp[:, 0:A],
                                in1=ropes, op=ALU.mult)
                return e.tensor_tensor(out=qt[:, 0:A], in0=qt[:, 0:A], in1=ropec, op=ALU.mult)
            em("dve", krope1, reads=[R_qtmp[0], R_rtmp, R_rope], writes=[R_qtmp[0], R_rtmp])
            em("dve", lambda e, kvh=kvh: e.tensor_tensor(
                out=kfm[:, A * kvh:A * (kvh + 1)], in0=qtmp[0][:, 0:A],
                in1=rtmp, op=ALU.add),
               reads=[R_qtmp[0], R_rtmp], writes=[R_k])

    wv3 = wv_buf.rearrange("p (kc f) -> p kc f", f=512)
    em("gpq", lambda e: e.dma_start(
        out=wv3, in_=Wv.rearrange("(kc p) f -> p kc f", p=128)),
       writes=[R_wstr[0], R_wstr[1]], key="kwv")
    for tt in range(NTT):
        bank = tt % 6
        pv = pm[:, 512 * bank:512 * (bank + 1)]

        def vmm(e, tt=tt, pv=pv):
            last = None
            for kc in range(NKC):
                last = e.matmul(pv, xnT3[:, kc, 128 * tt:128 * (tt + 1)],
                                wv3[:, kc, :], start=(kc == 0), stop=(kc == NKC - 1))
            return last
        em("pe", vmm, reads=[R_wstr[0], R_wstr[1], R_xnT_all], writes=[PB[bank]])
        em("act", lambda e, tt=tt, pv=pv: e.activation(
            out=vtm[:, 512 * tt:512 * (tt + 1)], in_=pv, func=AF.Identity),
           reads=[PB[bank]], writes=[R_v])

    if debug:
        em("syncq", lambda e: e.dma_start(out=dbg["dbg_q"], in_=qfm), reads=R_q, key="kdbg")
        em("syncq", lambda e: e.dma_start(out=dbg["dbg_k"], in_=kfm), reads=[R_k], key="kdbg")
        em("syncq", lambda e: e.dma_start(out=dbg["dbg_v"], in_=vtm), reads=[R_v], key="kdbg")

    em("pool", lambda e: e.memset(wts, 0.0), writes=[R_wts])
    insts = [(qb, h, j) for qb in range(2) for h in range(NH) for j in range(4)]
    N = len(insts)
    NDBG = 5
    dbg_sc_t = sb("dbg_sc_t", [128, 8], F32, SM + 2176) if debug else None

    def sp_of(par):
        b0 = 0 if par == 0 else 1536
        return pm[:, b0:b0 + 1280], ([PB[0], PB[1], PB[2]] if par == 0 else [PB[3], PB[4], PB[5]])

    WIN_SPLITS = [(0, 0, 512), (0, 512, 128), (640, 0, 384), (640, 384, 256)]

    def emit_scores(n):
        qb, h, j = insts[n]
        qt = 4 * qb + j
        par = n % 2
        sp, spb = sp_of(par)
        kvh = h // 4
        k0 = 128 * qt
        q0 = T * h + 128 * qt

        def f(e, q0=q0, kvh=kvh, k0=k0, sp=sp):
            last = None
            for sbase, koff, nk in WIN_SPLITS:
                dlo = 0 if sbase == 0 else SUB
                last = e.matmul(
                    sp[:, sbase + koff:sbase + koff + nk],
                    qfm[dlo:dlo + SUB, q0:q0 + 128],
                    kfm[dlo:dlo + SUB, A * kvh + k0 + koff:A * kvh + k0 + koff + nk],
                    start=True, stop=True)
            return last
        em("pe", f, reads=[R_q[h], R_k], writes=spb)

    def emit_mask(n):
        qb, h, j = insts[n]
        qt = 4 * qb + j
        par = n % 2
        sp, spb = sp_of(par)

        def f(e, qt=qt, sp=sp):
            spv = sp.rearrange("p (s c) -> p s c", c=640)
            if qt < 4:
                L = 512 - 128 * qt
                sv = spv[:, :, 0:L]
                cv = combo[:, 0:L].unsqueeze(1).broadcast_to([128, 2, L])
                e.tensor_tensor(out=sv, in0=sv, in1=cv, op=ALU.add)
            else:
                sv = spv[:, :, 0:128]
                bv = band[:, 0:128].unsqueeze(1).broadcast_to([128, 2, 128])
                e.tensor_tensor(out=sv, in0=sv, in1=bv, op=ALU.add)
            sv2 = spv[:, :, 512:640]
            bv2 = band[:, 512:640].unsqueeze(1).broadcast_to([128, 2, 128])
            return e.tensor_tensor(out=sv2, in0=sv2, in1=bv2, op=ALU.add)
        em("dve", f, reads=[spb, R_band, R_combo], writes=spb)

    def emit_exp(n):
        par = n % 2
        p4 = n % 4
        sp, spb = sp_of(par)
        em("act", lambda e, sp=sp, par=par, p4=p4: e.activation(
            out=ebuf[par][:, 0:640], in_=sp[:, 0:640], func=AF.Exp,
            bias=expb, scale=SCALE, accum_out=S1[p4]),
           reads=spb + [R_sm], writes=[R_e[par], R_S1[p4]])
        em("act", lambda e, sp=sp, par=par, p4=p4: e.activation(
            out=ebuf[par][:, 640:1280], in_=sp[:, 640:1280], func=AF.Exp,
            bias=expb, scale=SCALE, accum_out=S2[p4]),
           reads=spb + [R_sm], writes=[R_e[par], R_S2[p4]])

    def emit_dump_e(n):
        par = n % 2
        em("syncq", lambda e, par=par: e.dma_start(out=dbg["dbg_e"], in_=ebuf[par]),
           reads=[R_e[par]], key="kdbg")

    def emit_stt(n):
        qb, h, j = insts[n]
        par = n % 2
        p4 = n % 4
        em("dve", lambda e, par=par, p4=p4: e.reciprocal(out=r2s[par], in_=S2[p4]),
           reads=[R_S2[p4]], writes=[R_r2[par]])
        em("dve", lambda e, par=par, p4=p4, h=h: e.tensor_scalar(
            out=ngc[par], in0=S1[p4], scalar1=r2s[par],
            scalar2=neglam[:, h:h + 1], op0=ALU.mult, op1=ALU.mult),
           reads=[R_S1[p4], R_r2[par], R_sm], writes=[R_ngc[par]])
        em("dve", lambda e, par=par: e.scalar_tensor_tensor(
            out=wp[par][:, 0:384], in0=ebuf[par][:, 640:1024], scalar=ngc[par],
            in1=ebuf[par][:, 0:384], op0=ALU.mult, op1=ALU.add),
           reads=[R_e[par], R_ngc[par]], writes=[R_wp[par]])
        em("pool", lambda e, par=par: e.tensor_scalar(
            out=pscr[par], in0=ebuf[par][:, 1024:1280], scalar1=ngc[par],
            scalar2=1.0, op0=ALU.mult, op1=ALU.mult),
           reads=[R_e[par], R_ngc[par]], writes=[R_pscr[par]])
        em("pool", lambda e, par=par: e.tensor_tensor(
            out=wp[par][:, 384:640], in0=pscr[par],
            in1=ebuf[par][:, 384:640], op=ALU.add),
           reads=[R_pscr[par], R_e[par]], writes=[R_wp[par]])

    def emit_relu(n):
        par = n % 2
        p4 = n % 4
        em("act", lambda e, par=par, p4=p4: e.activation(
            out=wbf[p4], in_=wp[par], func=AF.Relu, accum_out=gg[par]),
           reads=[R_wp[par]], writes=[R_wbf[p4], R_gg[par]])

    def emit_tds(n):
        par = n % 2
        p4 = n % 4
        em("dve", lambda e, par=par, p4=p4: e.tensor_scalar(
            out=tds[par], in0=S1[p4], scalar1=1e-6,
            scalar2=gg[par], op0=ALU.mult, op1=ALU.add),
           reads=[R_S1[p4], R_gg[par]], writes=[R_tds[par]])

    def emit_tds2(n):
        par = n % 2
        em("act", lambda e, par=par: e.activation(out=tds2[par], in_=tds[par], func=AF.Identity),
           reads=[R_tds[par]], writes=[R_tds[par]])

    def emit_diffn(n):
        par = n % 2
        p4 = n % 4
        em("dve", lambda e, par=par: e.reciprocal(out=rv2[par], in_=tds2[par]),
           reads=[R_tds[par]], writes=[R_rv2[par]])
        em("pool", lambda e, par=par, p4=p4: e.tensor_scalar(
            out=dif[par], in0=wbf[p4], scalar1=rv2[par], scalar2=1.0,
            op0=ALU.mult, op1=ALU.mult),
           reads=[R_wbf[p4], R_rv2[par]], writes=[R_dif[par]])

    def emit_dump_small(n):
        par = n % 2

        def f(e, par=par):
            e.tensor_copy(out=dbg_sc_t[:, 0:1], in_=S1[par])
            e.tensor_copy(out=dbg_sc_t[:, 1:2], in_=S2[par])
            e.tensor_copy(out=dbg_sc_t[:, 2:3], in_=gg[par])
            e.tensor_copy(out=dbg_sc_t[:, 3:4], in_=r2s[par])
            e.tensor_copy(out=dbg_sc_t[:, 4:5], in_=ngc[par])
            e.tensor_copy(out=dbg_sc_t[:, 5:6], in_=rv2[par])
            return e.tensor_copy(out=dbg_sc_t[:, 6:7], in_=tds[par])
        em("dve", f, reads=[R_S1[par], R_S2[par], R_gg[par], R_r2[par], R_ngc[par],
                            R_rv2[par], R_tds[par]], writes=[R_sm])
        em("syncq", lambda e: e.dma_start(out=dbg["dbg_sc"], in_=dbg_sc_t), reads=[R_sm], key="kdbg")
        em("syncq", lambda e, par=par: e.dma_start(out=dbg["dbg_wp"], in_=wp[par]),
           reads=[R_wp[par]], key="kdbg")
        em("syncq", lambda e, par=par: e.dma_start(out=dbg["dbg_dif"], in_=dif[par]),
           reads=[R_dif[par]], key="kdbg")

    wtp = pm[:, 3584:3904].bitcast(BF16)
    wts3 = wts.rearrange("p (t c) -> p t c", c=512)

    def emit_transp(n):
        qb, h, j = insts[n]
        par = n % 2

        def f(e, par=par):
            last = None
            for k in range(5):
                last = e.transpose(wtp[:, 128 * k:128 * (k + 1)],
                                   dif[par][:, 128 * k:128 * (k + 1)], ident)
            return last
        em("pe", f, reads=[R_dif[par], R_sm], writes=[PB[7]])

    def emit_wtevac(n):
        qb, h, j = insts[n]
        em("dve", lambda e, j=j: e.tensor_copy(
            out=wts3[:, j:j + 5, 128 * j:128 * (j + 1)],
            in_=wtp.rearrange("p (t c) -> p t c", c=128)),
           reads=[PB[7]], writes=[R_wts])

    pav = pm[:, 3072:3584]

    def emit_av(n):
        qb, h, j = insts[n]
        assert j == 3
        kvh = h // 4

        def f(e, qb=qb, kvh=kvh):
            last = None
            for i in range(8):
                kc_abs = 4 * qb + i
                last = e.matmul(pav, vtm[:, 512 * kc_abs + 128 * kvh:512 * kc_abs + 128 * kvh + 128],
                                wts3[:, i, :], start=(i == 0), stop=(i == 7))
            return last
        em("pe", f, reads=[R_wts, R_v], writes=[PB[6]])

    def emit_avevac(n):
        qb, h, j = insts[n]
        em("act", lambda e, qb=qb, h=h: e.activation(
            out=attn[:, T * h + 512 * qb:T * h + 512 * qb + 512], in_=pav, func=AF.Identity),
           reads=[PB[6]], writes=[R_attn[h]])

    wo3 = [wo_buf[i].rearrange("p (kc f) -> p kc f", f=256) for i in range(2)]
    dma_wo = stream_w(wo_buf, R_wstr, Wo, 8, 256)
    dma_wo(0)
    dma_wo(1)

    for n in range(N + 14):
        if 0 <= n - 10 < N and insts[n - 10][2] == 3:
            emit_avevac(n - 10)
        if 0 <= n - 2 < N:
            emit_exp(n - 2)
            if debug and n - 2 == NDBG:
                emit_dump_e(n - 2)
        if n < N:
            emit_scores(n)
        if 0 <= n - 1 < N:
            emit_mask(n - 1)
        if 0 <= n - 3 < N:
            emit_stt(n - 3)
        if 0 <= n - 4 < N:
            emit_relu(n - 4)
        if 0 <= n - 5 < N:
            emit_tds(n - 5)
        if 0 <= n - 6 < N:
            emit_tds2(n - 6)
        if 0 <= n - 8 < N:
            emit_wtevac(n - 8)
            if insts[n - 8][2] == 3:
                emit_av(n - 8)
        if 0 <= n - 7 < N:
            emit_diffn(n - 7)
            emit_transp(n - 7)
            if debug and n - 7 == NDBG:
                emit_dump_small(n - 7)
        if 0 <= n - 13 < N and insts[n - 13][2] == 3:
            if debug and insts[n - 13] == (NDBG // 64, (NDBG // 4) % 16, 3):
                em("syncq", lambda e: e.dma_start(out=dbg["dbg_wts"], in_=wts),
                   reads=[R_wts], key="kdbg")

    if debug:
        em("syncq", lambda e: e.dma_start(out=dbg["dbg_attn"], in_=attn), reads=R_attn, key="kdbg")

    dma_w1 = stream_w(w1_buf, R_wstr, W1T, 16, 256)
    for cb in range(8):
        bi = cb % 2
        for sub in range(2):
            dc = 2 * cb + sub
            par = dc % 2
            pq = pm[:, 3072:4096] if par == 0 else pm[:, 0:1024]
            pqb = [PB[6], PB[7]] if par == 0 else [PB[0], PB[1]]

            def womm(e, bi=bi, sub=sub, pq=pq):
                last = None
                for kc in range(NKC):
                    for w in range(2):
                        last = e.matmul(pq[:, 512 * w:512 * (w + 1)],
                                        wo3[bi][:, kc, 128 * sub:128 * (sub + 1)],
                                        attn[:, T * kc + 512 * w:T * kc + 512 * w + 512],
                                        start=(kc == 0), stop=(kc == NKC - 1))
                return last
            em("pe", womm, reads=[R_wstr[bi]] + R_attn, writes=pqb)
            em("act", lambda e, dc=dc, pq=pq: e.activation(
                out=outac[:, T * dc:T * (dc + 1)], in_=pq, func=AF.Identity,
                bias=b3s[:, dc:dc + 1]),
               reads=pqb + [R_sm], writes=[R_out[dc]])
        if cb + 2 < 8:
            dma_wo(cb + 2)
        elif cb == 6:
            dma_w1(0)
        elif cb == 7:
            dma_w1(1)

    if debug:
        em("syncq", lambda e: e.dma_start(out=dbg["dbg_wo"], in_=outac), reads=R_out, key="kdbg")

    w13 = [w1_buf[i].rearrange("p (kc f) -> p kc f", f=256) for i in range(2)]
    for cb in range(16):
        bi = cb % 2
        for sub in range(2):
            fc = 2 * cb + sub
            par = fc % 2
            pq = pm[:, 3072:4096] if par == 0 else pm[:, 0:1024]
            pqb = [PB[6], PB[7]] if par == 0 else [PB[0], PB[1]]

            def w1mm(e, bi=bi, sub=sub, pq=pq):
                last = None
                for kc in range(NKC):
                    for w in range(2):
                        last = e.matmul(pq[:, 512 * w:512 * (w + 1)],
                                        w13[bi][:, kc, 128 * sub:128 * (sub + 1)],
                                        xnT3[:, kc, HALO + 512 * w:HALO + 512 * w + 512],
                                        start=(kc == 0), stop=(kc == NKC - 1))
                return last
            em("pe", w1mm, reads=[R_wstr[bi], R_xnT_all], writes=pqb)
            em("act", lambda e, fc=fc, pq=pq: e.activation(
                out=h1[:, T * fc:T * (fc + 1)], in_=pq, func=AF.Relu,
                bias=b1s[:, fc:fc + 1]),
               reads=pqb + [R_sm], writes=[R_h1[fc]])
        if cb + 2 < 16:
            dma_w1(cb + 2)

    if debug:
        em("syncq", lambda e: e.dma_start(out=dbg["dbg_h1"], in_=h1), reads=R_h1, key="kdbg")

    w23 = [w2_buf[i].rearrange("p (kc f) -> p kc f", f=128) for i in range(2)]
    w33 = [w3_buf[i].rearrange("p (kc f) -> p kc f", f=256) for i in range(2)]
    for blk in range(2):
        def dma_w2(ci):
            bi = ci % 2
            for half in range(2):
                em("gpq", lambda e, ci=ci, bi=bi, blk=blk, half=half: e.dma_start(
                    out=w23[bi][:, 16 * half:16 * (half + 1), :],
                    in_=W2T[2048 * half:2048 * (half + 1),
                            2048 * blk + 128 * ci:2048 * blk + 128 * (ci + 1)]
                    .rearrange("(kc p) f -> p kc f", p=128)),
                   writes=[R_w2[bi]], key=f"k2{bi}")
        dma_w2(0); dma_w2(1)
        for fc2 in range(16):
            bi = fc2 % 2
            par = fc2 % 2
            pq = pm[:, 3072:4096] if par == 0 else pm[:, 0:1024]
            pqb = [PB[6], PB[7]] if par == 0 else [PB[0], PB[1]]

            def w2mm(e, bi=bi, pq=pq):
                last = None
                for kc in range(32):
                    for w in range(2):
                        last = e.matmul(pq[:, 512 * w:512 * (w + 1)],
                                        w23[bi][:, kc, :],
                                        h1[:, T * kc + 512 * w:T * kc + 512 * w + 512],
                                        start=(kc == 0), stop=(kc == 31))
                return last
            em("pe", w2mm, reads=[R_w2[bi]] + R_h1, writes=pqb)
            em("act", lambda e, fc2=fc2, blk=blk, pq=pq: e.activation(
                out=h2[:, T * fc2:T * (fc2 + 1)], in_=pq, func=AF.Relu,
                bias=b2s[:, 16 * blk + fc2:16 * blk + fc2 + 1]),
               reads=pqb + [R_sm], writes=[R_h2[fc2]])
            if fc2 + 2 < 16:
                dma_w2(fc2 + 2)

        def dma_w3(ci):
            bi = ci % 2
            em("gpq", lambda e, ci=ci, bi=bi, blk=blk: e.dma_start(
                out=w33[bi],
                in_=W3T[2048 * blk:2048 * (blk + 1), 256 * ci:256 * (ci + 1)]
                .rearrange("(kc p) f -> p kc f", p=128)),
               writes=[R_w3[bi]], key=f"k3{bi}")
        dma_w3(0); dma_w3(1)
        for cb in range(8):
            bi = cb % 2
            for sub in range(2):
                dc = 2 * cb + sub
                par = dc % 2
                pq = pm[:, 1024:2048] if par == 0 else pm[:, 2048:3072]
                pqb = [PB[2], PB[3]] if par == 0 else [PB[4], PB[5]]

                def w3mm(e, bi=bi, sub=sub, pq=pq):
                    last = None
                    for kc in range(16):
                        for w in range(2):
                            last = e.matmul(pq[:, 512 * w:512 * (w + 1)],
                                            w33[bi][:, kc, 128 * sub:128 * (sub + 1)],
                                            h2[:, T * kc + 512 * w:T * kc + 512 * w + 512],
                                            start=(kc == 0), stop=(kc == 15))
                    return last
                em("pe", w3mm, reads=[R_w3[bi]] + R_h2, writes=pqb)
                em("dve", lambda e, dc=dc, pq=pq: e.tensor_tensor(
                    out=outac[:, T * dc:T * (dc + 1)], in0=pq,
                    in1=outac[:, T * dc:T * (dc + 1)], op=ALU.add),
                   reads=pqb + [R_out[dc]], writes=[R_out[dc]])
            if cb + 2 < 8:
                dma_w3(cb + 2)

    for tt in range(8):
        par = tt % 2
        xt = xf[par]
        rxt = R_wstr[par]
        em("syncq", lambda e, tt=tt, xt=xt: e.dma_start(
            out=xt, in_=xh[HALO + 128 * tt:HALO + 128 * (tt + 1), :]),
           writes=[rxt], key=f"kx{tt % 2}")
        pf = pm[:, 0:2048] if par == 0 else pm[:, 2048:4096]
        pfb = [PB[0], PB[1], PB[2], PB[3]] if par == 0 else [PB[4], PB[5], PB[6], PB[7]]

        def ftr(e, tt=tt, pf=pf):
            last = None
            for dc in range(NKC):
                last = e.transpose(pf[:, 128 * dc:128 * (dc + 1)],
                                   outac[:, T * dc + 128 * tt:T * dc + 128 * (tt + 1)],
                                   ident32)
            return last
        em("pe", ftr, reads=R_out + [R_sm], writes=pfb)
        em("dve", lambda e, par=par, pf=pf, xt=xt: e.tensor_tensor(
            out=outs[par], in0=pf, in1=xt, op=ALU.add),
           reads=pfb + [rxt], writes=[R_outs[par]])
        em("syncq", lambda e, tt=tt, par=par: e.dma_start(
            out=y[128 * tt:128 * (tt + 1), :], in_=outs[par]),
           reads=[R_outs[par]], key=f"kyo{par}")

    s.sem("kyo0"); s.sem("kyo1")
    fin0, fin1 = s.cnt["kyo0"], s.cnt["kyo1"]

    with nc.Block() as block:
        @block.sync
        def _(e):
            s.replay(("syncq",), e)
            e.wait_ge(s.sems["kyo0"], fin0)
            e.wait_ge(s.sems["kyo1"], fin1)

        @block.vector
        def _(e):
            s.replay(("dve",), e)

        @block.scalar
        def _(e):
            s.replay(("act",), e)

        @block.tensor
        def _(e):
            s.replay(("pe",), e)

        @block.gpsimd
        def _(e):
            s.replay(("pool", "gpq"), e)

    return nc



def host_inputs(x_full, Wq, Wk, Wv, Wo, lam, W1, b1, W2, b2, W3, b3):
    import ml_dtypes
    f32 = np.float32
    xr = np.ascontiguousarray(x_full, dtype=f32).reshape(NCORES * T, D)
    W1T = np.ascontiguousarray(np.asarray(W1, f32).T)
    W2T = np.ascontiguousarray(np.asarray(W2, f32).T)
    W3T = np.ascontiguousarray(np.asarray(W3, f32).T)
    b1t = np.ascontiguousarray(np.asarray(b1, f32).reshape(32, 128).T)
    b2t = np.ascontiguousarray(np.asarray(b2, f32).reshape(32, 128).T)
    b3t = np.ascontiguousarray(np.asarray(b3, f32).reshape(16, 128).T)
    lam_b = np.ascontiguousarray(np.broadcast_to(np.asarray(lam, f32)[None, :], (128, 16)))

    r = np.arange(128)[:, None]
    kk = np.arange(640)[None, :]
    bandv = np.where((kk - r >= 1) & (kk - r <= WIN), 0.0, NEG).astype(f32)
    band = np.concatenate([bandv, bandv], axis=1).astype(ml_dtypes.bfloat16)

    maps = []
    for c in range(NCORES):
        sq = c % 4
        t0 = sq * T
        own = xr[c * T:(c + 1) * T]
        if sq == 0:
            halo = np.zeros((HALO, D), f32)
        else:
            halo = xr[c * T - HALO:c * T]
        xh = np.concatenate([halo, own], axis=0)

        pos = (np.arange(t0 - HALO, t0 + T, dtype=f32))[None, :]
        invf = (f32(1.0) / (f32(10000.0) ** (np.arange(0, HD, 2, dtype=f32) / f32(HD))))[:, None]
        th = invf * pos
        cosv = np.cos(np.concatenate([th, th], 0)).astype(f32)
        sinv = np.sin(np.concatenate([th, th], 0)).astype(f32)
        sinm = sinv.copy()
        sinm[0:64] *= -1.0
        if sq == 0:
            cosv[:, 0:HALO] = 1.0
            sinm[:, 0:HALO] = 0.0

        combov = np.zeros((128, 512), f32)
        if sq == 0:
            combov[:, :] = NEG
        else:
            combov[:, 0:128] = bandv[:, 0:128]
        maps.append(dict(
            xh=xh, Wq=Wq, Wk=Wk, Wv=Wv, Wo=Wo, W1T=W1T, W2T=W2T, W3T=W3T,
            b1t=b1t, b2t=b2t, b3t=b3t, lam_bc=lam_b,
            ropecos=np.ascontiguousarray(cosv), ropesinm=np.ascontiguousarray(sinm),
            band=band, combo=combov.astype(ml_dtypes.bfloat16),
        ))
    return maps


def kernel(L0_attn_x, L0_attn_Wq, L0_attn_Wk, L0_attn_Wv, L0_attn_Wo, L0_attn_lam,
           L0_ffn_W1, L0_ffn_b1, L0_ffn_W2, L0_ffn_b2, L0_ffn_W3, L0_ffn_b3):
    f32 = np.float32
    maps = host_inputs(np.asarray(L0_attn_x, f32), np.asarray(L0_attn_Wq, f32),
                       np.asarray(L0_attn_Wk, f32), np.asarray(L0_attn_Wv, f32),
                       np.asarray(L0_attn_Wo, f32), np.asarray(L0_attn_lam, f32),
                       np.asarray(L0_ffn_W1, f32), np.asarray(L0_ffn_b1, f32),
                       np.asarray(L0_ffn_W2, f32), np.asarray(L0_ffn_b2, f32),
                       np.asarray(L0_ffn_W3, f32), np.asarray(L0_ffn_b3, f32))
    nc = build()
    res = run_bass_kernel_spmd(nc, maps, core_ids=list(range(NCORES)))
    out = np.concatenate([r["y"] for r in res.results], axis=0)
    return out.reshape(2, 4096, D)


if __name__ == "__main__":
    nc = build()
    print("built ok")
